# version 28
# baseline (speedup 1.0000x reference)
import contextlib
import numpy as np

import concourse.bass as bass
import concourse.mybir as mybir
from concourse.bass_utils import run_bass_kernel_spmd

F32 = mybir.dt.float32
BF16 = mybir.dt.bfloat16
I32 = mybir.dt.int32
AF = mybir.ActivationFunctionType
OP = mybir.AluOpType
AX = mybir.AxisListType

D = 1024
L = 2048
H = 16
NCH = D // 128
NTT = L // 128
VW = 65
VAW = H * VW
SCALE = 0.125
NCORES = 8
BITERS = 22
GROUPS = [(o, t) for o in range(NCH) for t in range(4)]


class Ctr:
    def __init__(self, sem, step=1):
        self.sem = sem
        self.v = 0
        self.step = step

    def inc(self, instr, n=1):
        instr.then_inc(self.sem, n * self.step)
        self.v += n * self.step
        return instr


def _build_nc_inner(stop=99, box=None):
    nc = bass.Bass("TRN2", target_bir_lowering=False)
    PE, DV, AC, GP, SY = nc.tensor, nc.vector, nc.scalar, nc.gpsimd, nc.sync

    class StopEmit(Exception):
        pass

    xT = nc.dram_tensor("xT", [D, L], F32, kind="ExternalInput").ap()
    wqT = nc.dram_tensor("wqT", [D, D], F32, kind="ExternalInput").ap()
    wkT = nc.dram_tensor("wkT", [D, D], F32, kind="ExternalInput").ap()
    wvT = nc.dram_tensor("wvT", [D, D], F32, kind="ExternalInput").ap()
    woT = nc.dram_tensor("woT", [D, D], F32, kind="ExternalInput").ap()
    wfT = nc.dram_tensor("wfT", [D, D], F32, kind="ExternalInput").ap()
    bq_c = nc.dram_tensor("bq_c", [128, NCH], F32, kind="ExternalInput").ap()
    bk_c = nc.dram_tensor("bk_c", [128, NCH], F32, kind="ExternalInput").ap()
    bo_c = nc.dram_tensor("bo_c", [128, NCH], F32, kind="ExternalInput").ap()
    bv_r = nc.dram_tensor("bv_r", [1, D], F32, kind="ExternalInput").ap()
    fb_r = nc.dram_tensor("fb_r", [1, D], F32, kind="ExternalInput").ap()
    cst32 = nc.dram_tensor("cst32", [128, 392], F32, kind="ExternalInput").ap()
    cstE = nc.dram_tensor("cstE", [16, 1024], BF16, kind="ExternalInput").ap()
    yT = nc.dram_tensor("yT", [D, L], F32, kind="ExternalOutput").ap()
    cc_in = nc.dram_tensor("cc_in", [128, 1], F32)
    cc_out = nc.dram_tensor("cc_out", [128, 1], F32)

    es = contextlib.ExitStack()
    if box is not None:
        box["nc"] = nc
        box["es"] = es
    SB = lambda n, s, d: es.enter_context(nc.sbuf_tensor(n, s, d))

    def SEM(name, step=1):
        return Ctr(es.enter_context(nc.semaphore(name)), step)

    s_ld = SEM("s_ld", 16)
    s_gd = SEM("s_gd", 16)
    s_mm = SEM("s_mm")
    s_vv = SEM("s_vv")
    s_aa = SEM("s_aa")
    s_cc = SEM("s_cc")
    s_st = SEM("s_st", 16)
    s_lx = [SEM(f"s_lx{i}", 16) for i in range(NCH)]
    s_lr = [SEM("s_lr0", 16), SEM("s_lr1", 16)]
    s_lf = SEM("s_lf", 16)
    s_gs = [SEM("s_gs0", 16), SEM("s_gs1", 16)]

    def dbg_dump(ap, rows, cols, gate_sems=()):
        for sem, val in gate_sems:
            GP.wait_ge(sem, val)
        ytf = yT.rearrange("a b -> (a b)").rearrange("(r c) -> r c", c=cols)
        s_gd.inc(GP.dma_start(out=ytf[0:rows, :], in_=ap))
        GP.wait_ge(s_gd.sem, s_gd.v)
        err = StopEmit()
        err.nc = None
        raise err

    c32 = SB("c32", [128, 392], F32)
    nc.const_aps.register(F32, 0.0, c32[:, 384:385])
    nc.const_aps.register(F32, 1e-5, c32[:, 385:386])
    nc.const_aps.register(F32, -0.5, c32[:, 386:387])
    nc.const_aps.register(F32, SCALE, c32[:, 387:388])
    nc.const_aps.register(F32, 1.0, c32[:, 0:1])
    ONES = c32[:, 0:128]
    B16 = c32[:, 128:256]
    IDT = c32[:, 256:384]
    cE = SB("cE", [16, 1024], BF16)
    bfpool = SB("bfpool", [128, 1552], BF16)
    bva = bfpool[0:1, 0:VAW]
    ONESBF = bfpool[0:1, VAW:VAW + 512]
    sbq = SB("sbq", [128, NCH], F32)
    sbk = SB("sbk", [128, NCH], F32)
    sbo = SB("sbo", [128, NCH], F32)

    kmaxv = SB("kmaxv", [16, 1], F32)
    sqq = SB("sqq", [16, 1], F32)
    sqk = SB("sqk", [16, 1], F32)
    negB = SB("negB", [16, 1], F32)
    thrb = c32[:, 388:389]
    wscb = c32[:, 389:390]
    wsum = c32[0:1, 390:391]
    thr1 = c32[0:1, 391:392]
    negBb = SB("negBb", [128, 16], F32)
    nthrb = negBb[:, 0:1]
    blo = [negBb[0:16, 4:5], negBb[0:16, 5:6]]
    bhi = [negBb[0:16, 6:7], negBb[0:16, 7:8]]
    btm = negBb[0:16, 8:9]
    bcount = negBb[0:16, 9:10]
    bcondf = negBb[0:16, 10:11]
    bcondn = negBb[0:16, 11:12]
    bt4 = negBb[0:16, 12:16]

    wbfA = SB("wbfA", [128, NCH * D], BF16)
    wbfB = SB("wbfB", [128, NCH * D], BF16)
    wbfC = SB("wbfC", [128, NCH * VAW], BF16)
    wstage = SB("wstage", [128, D], F32)

    for dst, srcd in [(c32, cst32), (cE, cstE), (sbq, bq_c),
                      (sbk, bk_c), (sbo, bo_c)]:
        s_ld.inc(SY.dma_start(out=dst[:], in_=srcd))
    cst_ld = s_ld.v
    SY.wait_ge(s_ld.sem, cst_ld)
    DV.wait_ge(s_ld.sem, cst_ld)
    DV.memset(bva[:], 1.0)
    DV.memset(ONESBF, 1.0)
    s_vv.inc(DV.drain())
    GP.wait_ge(s_vv.sem, s_vv.v)
    s_gd.inc(GP.dma_start(
        out=bva.rearrange("a (h w) -> a h w", h=H)[:, :, 0:64],
        in_=bv_r.rearrange("a (h w) -> a h w", h=H)))
    bva_gd = s_gd.v

    def cast_plain(dst_bf, c):
        return DV.tensor_copy(out=dst_bf[:, D * c:D * (c + 1)], in_=wstage[:])

    def cast_vaug(dst_bf, c):
        dv = dst_bf[:, VAW * c:VAW * (c + 1)].rearrange(
            "p (h w) -> p h w", h=H)[:, :, 0:64]
        return DV.tensor_copy(
            out=dv, in_=wstage[:].rearrange("p (h w) -> p h w", h=H))

    def emit_stage(dram, dst, castfn, gate=None, hook=None):
        wv_ = dram.rearrange("(c p) o -> p c o", p=128)
        for q in range(NCH):
            SY.wait_ge(s_vv.sem, s_vv.v)
            s_ld.inc(SY.dma_start(out=wstage[:], in_=wv_[:, q, :]))
            DV.wait_ge(s_ld.sem, s_ld.v)
            if gate is not None and q == 0:
                DV.wait_ge(gate[0], gate[1])
            s_vv.inc(castfn(dst, q))
            if hook is not None:
                hook(q)
        return s_vv.v

    _ln_tag = [0]

    def emit_ln(src_of, gate_vv, scr, xnb, psum_scope, x_gate=None):
        _ln_tag[0] += 1
        tag = str(_ln_tag[0])
        """Computes rstd_b/mr_b [128, L] psum tiles for LN(src).
        src_of(c): [128, L] f32 slice. scr: [128, L] f32 scratch.
        xnb: [128, L] bf16 scratch. psum_scope: ExitStack owning the
        returned p_rb/p_mb tiles. Returns (p_rb, p_mb, vv_ready)."""
        aa0 = s_aa.v
        mm0 = s_mm.v
        with contextlib.ExitStack() as esP:
            p_s = esP.enter_context(nc.psum_tensor("p_s" + tag, [1, L], F32))
            p_q = esP.enter_context(nc.psum_tensor("p_q" + tag, [1, L], F32))
            for c in range(NCH):
                if x_gate is not None:
                    x_gate(c, AC)
                else:
                    AC.wait_ge(s_vv.sem, gate_vv)
                if c >= 1:
                    AC.wait_ge(s_mm.sem, mm0 + c)
                s_aa.inc(AC.activation(out=scr[:], in_=src_of(c),
                                       func=AF.Square))
            for c in range(NCH):
                if x_gate is not None:
                    x_gate(c, PE)
                else:
                    PE.wait_ge(s_vv.sem, gate_vv)
                for s in range(4):
                    PE.matmul(p_s[0:1, 512 * s:512 * (s + 1)], ONES[:, 0:1],
                              src_of(c)[:, 512 * s:512 * (s + 1)],
                              start=(c == 0), stop=(c == NCH - 1))
                PE.wait_ge(s_aa.sem, aa0 + c + 1)
                for s in range(4):
                    mm = PE.matmul(p_q[0:1, 512 * s:512 * (s + 1)],
                                   ONES[:, 0:1],
                                   scr[:, 512 * s:512 * (s + 1)],
                                   start=(c == 0), stop=(c == NCH - 1))
                s_mm.inc(mm)
            DV.wait_ge(s_mm.sem, s_mm.v)
            DV.tensor_copy(out=scr[0:1, :], in_=p_s[0:1, :])
            s_vv.inc(DV.tensor_copy(out=scr[32:33, :], in_=p_q[0:1, :]))
        p_rb = psum_scope.enter_context(
            nc.psum_tensor("p_rb" + tag, [128, L], F32))
        p_mb = psum_scope.enter_context(
            nc.psum_tensor("p_mb" + tag, [128, L], F32))
        PE.wait_ge(s_vv.sem, s_vv.v)
        for s in range(4):
            PE.matmul(p_rb[:, 512 * s:512 * (s + 1)], ONES[0:1, :],
                      scr[0:1, 512 * s:512 * (s + 1)], start=True, stop=True)
            mm = PE.matmul(p_mb[:, 512 * s:512 * (s + 1)], ONES[32:33, :],
                          scr[32:33, 512 * s:512 * (s + 1)], start=True,
                          stop=True)
        s_mm.inc(mm)
        DV.wait_ge(s_mm.sem, s_mm.v)
        DV.tensor_scalar(out=scr[:], in0=p_rb[:], scalar1=1.0 / D,
                         scalar2=None, op0=OP.mult)
        DV.drain()
        DV.tensor_tensor(out=xnb[:], in0=scr[:], in1=scr[:], op=OP.mult)
        DV.drain()
        s_vv.inc(DV.scalar_tensor_tensor(
            out=p_rb[:], in0=p_mb[:], scalar=1.0 / D,
            in1=xnb[:], op0=OP.mult, op1=OP.subtract))
        AC.wait_ge(s_vv.sem, s_vv.v)
        AC.activation(out=p_mb[:], in_=p_rb[:], func=AF.Ln, bias=1e-5)
        AC.drain()
        s_aa.inc(AC.activation(out=p_rb[:], in_=p_mb[:], func=AF.Exp,
                               scale=-0.5))
        DV.wait_ge(s_aa.sem, s_aa.v)
        s_vv.inc(DV.tensor_tensor(out=p_mb[:], in0=scr[:], in1=p_rb[:],
                                  op=OP.mult))
        return p_rb, p_mb, s_vv.v

    def emit_proj(wbf, bias_col, dst, p_pr, q2s=None, p_norm=None,
                  first_norm_gate=None):
        aa_b = s_aa.v
        vv_b = s_vv.v
        PE.wait_ge(s_vv.sem, vv_b)
        PE.wait_ge(s_aa.sem, aa_b)
        proj_mm = []
        for i, (o, t) in enumerate(GROUPS):
            if i >= 2:
                if p_norm is not None:
                    PE.wait_ge(s_aa.sem, aa_b + (i - 2) + 1)
                PE.wait_ge(s_vv.sem, vv_b + (i - 2) + 1)
            for c in range(NCH):
                mm = PE.matmul(
                    p_pr[i % 2][:],
                    wbf[:, D * c + 128 * o:D * c + 128 * (o + 1)],
                    xn[:, L * c + 512 * t:L * c + 512 * (t + 1)],
                    start=(c == 0), stop=(c == NCH - 1))
            s_mm.inc(mm)
            proj_mm.append(s_mm.v)
            if p_norm is not None and i >= 1:
                PE.wait_ge(s_aa.sem, aa_b + i)
                if i == 1 and first_norm_gate is not None:
                    PE.wait_ge(first_norm_gate[0], first_norm_gate[1])
                oo, tt = GROUPS[i - 1]
                mm = PE.matmul(p_norm[:, 512 * tt:512 * (tt + 1)],
                               B16[:, 16 * oo:16 * (oo + 1)],
                               q2s[(i - 1) % 2][:],
                               start=(oo == 0), stop=(oo == NCH - 1))
                s_mm.inc(mm)
        if p_norm is not None:
            PE.wait_ge(s_aa.sem, aa_b + len(GROUPS))
            oo, tt = GROUPS[-1]
            mm = PE.matmul(p_norm[:, 512 * tt:512 * (tt + 1)],
                           B16[:, 16 * oo:16 * (oo + 1)],
                           q2s[(len(GROUPS) - 1) % 2][:],
                           start=(oo == 0), stop=(oo == NCH - 1))
            s_mm.inc(mm)
            for i, (o, t) in enumerate(GROUPS):
                AC.wait_ge(s_mm.sem, proj_mm[i])
                s_aa.inc(AC.activation(out=q2s[i % 2][:], in_=p_pr[i % 2][:],
                                       func=AF.Square,
                                       bias=bias_col[:, o:o + 1]))
        for i, (o, t) in enumerate(GROUPS):
            DV.wait_ge(s_mm.sem, proj_mm[i])
            if p_norm is not None:
                DV.wait_ge(s_aa.sem, aa_b + i + 1)
            s_vv.inc(DV.tensor_scalar(
                out=dst[:, L * o + 512 * t:L * o + 512 * (t + 1)],
                in0=p_pr[i % 2][:], scalar1=bias_col[:, o:o + 1],
                scalar2=None, op0=OP.add))
        return s_vv.v

    scope1 = contextlib.ExitStack()
    if box is not None:
        box["scope1"] = scope1
    xn = scope1.enter_context(nc.sbuf_tensor("xn", [128, NCH * L], BF16))
    with contextlib.ExitStack() as esA:
        xs = esA.enter_context(nc.sbuf_tensor("xs", [128, NCH * L], F32))
        sqscrA = esA.enter_context(nc.sbuf_tensor("sqscrA", [128, L], F32))
        xv = xT.rearrange("(c p) t -> p c t", p=128)
        for c in range(NCH):
            s_lx[c].inc(SY.dma_start(out=xs[:, L * c:L * (c + 1)],
                                     in_=xv[:, c, :]))
        wq_done = emit_stage(wqT, wbfA, cast_plain)

        def x_gate(c, eng):
            eng.wait_ge(s_lx[c].sem, 16)

        with contextlib.ExitStack() as esLP:
            p_rb, p_mb, ln1_rdy = emit_ln(
                lambda c: xs[:, L * c:L * (c + 1)], 0, sqscrA,
                xn[:, 0:L], esLP, x_gate=x_gate)
            DV.wait_ge(s_vv.sem, ln1_rdy)
            DV.drain()
            for c in range(NCH):
                DV.tensor_tensor(out=sqscrA[:], in0=xs[:, L * c:L * (c + 1)],
                                 in1=p_rb[:], op=OP.mult)
                DV.drain()
                DV.tensor_tensor(out=xn[:, L * c:L * (c + 1)], in0=sqscrA[:],
                                 in1=p_mb[:], op=OP.subtract)
                s_vv.inc(DV.drain())
            ln1_done = s_vv.v
            if stop == 1:
                dbg_dump(xn[:, 0:4096], 128, 4096,
                         [(s_vv.sem, ln1_done)])

    with contextlib.ExitStack() as esB:
        qf = esB.enter_context(nc.sbuf_tensor("qf", [128, NCH * L], BF16))
        kf = esB.enter_context(nc.sbuf_tensor("kf", [128, NCH * L], BF16))
        vt = esB.enter_context(nc.sbuf_tensor("vt", [128, NTT * VAW], BF16))
        shscr = esB.enter_context(nc.sbuf_tensor("shscr", [128, L], F32))
        q2s = [shscr[:, 0:512], shscr[:, 512:1024]]
        srow = shscr[32:33, 0:512]
        negBr = shscr[64:65, 512:528]
        mask16 = esB.enter_context(nc.sbuf_tensor("mask16", [16, L], BF16))
        esPR = contextlib.ExitStack()
        p_pr = [esPR.enter_context(nc.psum_tensor(f"p_pr{i}", [128, 512], F32))
                for i in range(2)]

        wk_done = emit_stage(wkT, wbfB, cast_plain)
        if stop == 15:
            dbg_dump(wbfB[:, 0:4096], 128, 4096, [(s_vv.sem, wk_done)])
        if stop == 16:
            PE.wait_ge(s_vv.sem, max(wq_done, ln1_done))
            emit_proj(wbfA, sbq, qf, p_pr)
            dbg_dump(qf[:, 0:4096], 128, 4096, [(s_vv.sem, s_vv.v)])
        if stop == 18:
            q2sD = [esB.enter_context(nc.sbuf_tensor(f"q2sE{i}", [128, 512], F32)) for i in range(2)]
            PE.wait_ge(s_vv.sem, max(wq_done, ln1_done))
            aa_b = s_aa.v
            vv_b = s_vv.v
            proj_mm = []
            for i, (o, t) in enumerate(GROUPS):
                if i >= 2:
                    PE.wait_ge(s_aa.sem, aa_b + (i - 2) + 1)
                    PE.wait_ge(s_vv.sem, vv_b + (i - 2) + 1)
                for c in range(NCH):
                    mm = PE.matmul(
                        p_pr[i % 2][:],
                        wbfA[:, D * c + 128 * o:D * c + 128 * (o + 1)],
                        xn[:, L * c + 512 * t:L * c + 512 * (t + 1)],
                        start=(c == 0), stop=(c == NCH - 1))
                s_mm.inc(mm)
                proj_mm.append(s_mm.v)
            for i, (o, t) in enumerate(GROUPS):
                AC.wait_ge(s_mm.sem, proj_mm[i])
                s_aa.inc(AC.activation(out=q2sD[i % 2][:], in_=p_pr[i % 2][:],
                                       func=AF.Square, bias=sbq[:, o:o + 1]))
            for i, (o, t) in enumerate(GROUPS):
                DV.wait_ge(s_mm.sem, proj_mm[i])
                DV.tensor_scalar(out=qf[:, L * o + 512 * t:L * o + 512 * (t + 1)],
                                 in0=p_pr[i % 2][:], scalar1=sbq[:, o:o + 1],
                                 scalar2=None, op0=OP.add)
                s_vv.inc(DV.drain())
            dbg_dump(qf[:, 0:4096], 128, 4096,
                     [(s_vv.sem, s_vv.v), (s_aa.sem, s_aa.v)])
        if stop == 17:
            with contextlib.ExitStack() as esP:
                p_nrm = esP.enter_context(nc.psum_tensor("p_nrmD", [16, L], F32))
                q2sD = [esB.enter_context(nc.sbuf_tensor(f"q2sD{i}", [128, 512], F32)) for i in range(2)]
                PE.wait_ge(s_vv.sem, max(wq_done, ln1_done))
                emit_proj(wbfA, sbq, qf, p_pr, q2s=q2sD, p_norm=p_nrm,
                          first_norm_gate=(s_vv.sem, ln1_done))
                dbg_dump(qf[:, 0:4096], 128, 4096,
                         [(s_vv.sem, s_vv.v), (s_mm.sem, s_mm.v)])

        normsT = esB.enter_context(nc.sbuf_tensor("normsT", [16, L], F32))
        junk16 = mask16

        with contextlib.ExitStack() as esP:
            p_nrm = esP.enter_context(nc.psum_tensor("p_nrm", [16, L], F32))
            PE.wait_ge(s_vv.sem, max(wq_done, ln1_done))
            emit_proj(wbfA, sbq, qf, p_pr, q2s=q2s, p_norm=p_nrm,
                      first_norm_gate=(s_vv.sem, ln1_done))
            q_mm_all = s_mm.v
            DV.wait_ge(s_mm.sem, q_mm_all)
            DV.tensor_copy(out=normsT[:], in_=p_nrm[:])
            s_vv.inc(DV.drain())
            norms_ready = s_vv.v
            if stop == 2:
                dbg_dump(qf[:, 0:4096], 128, 4096,
                         [(s_vv.sem, s_vv.v)])
            if stop == 25:
                dbg_dump(normsT[:], 16, L, [(s_vv.sem, norms_ready)])

        with contextlib.ExitStack() as esP:
            p_kn = esP.enter_context(nc.psum_tensor("p_kn", [16, L], F32))
            PE.wait_ge(s_vv.sem, wk_done)
            emit_proj(wbfB, sbk, kf, p_pr, q2s=q2s, p_norm=p_kn,
                      first_norm_gate=(s_vv.sem, norms_ready))
            k_mm_all = s_mm.v
            DV.wait_ge(s_mm.sem, k_mm_all)
            DV.reduce_max(out=kmaxv[:], in_=p_kn[:], axis=AX.X)
            s_vv.inc(DV.drain())
            kmax_ready = s_vv.v
            if stop == 3:
                dbg_dump(kf[:, 0:4096], 128, 4096, [(s_vv.sem, s_vv.v)])
        esPR.close()

        DV.wait_ge(s_vv.sem, norms_ready)
        DV.reduce_max(out=bhi[0][:], in_=normsT[:], axis=AX.X)
        DV.memset(blo[0][:], 0.0)
        DV.drain()
        DV.tensor_copy(out=sqq[:], in_=bhi[0][:])
        DV.drain()

        def emit_bisect_iter(it):
            a, b = it % 2, (it + 1) % 2
            DV.tensor_scalar(out=btm[:], in0=blo[a][:], scalar1=bhi[a][:],
                             scalar2=0.5, op0=OP.add, op1=OP.mult)
            DV.drain()
            DV.tensor_scalar(out=junk16[:], in0=normsT[:], scalar1=btm[:],
                             scalar2=None, op0=OP.is_gt, op1=OP.add,
                             accum_out=bcount[:])
            DV.drain()
            DV.tensor_scalar(out=bcondf[:], in0=bcount[:], scalar1=409.5,
                             scalar2=None, op0=OP.is_ge)
            DV.tensor_scalar(out=bcondn[:], in0=bcount[:], scalar1=409.5,
                             scalar2=None, op0=OP.is_lt)
            DV.drain()
            DV.tensor_tensor(out=bt4[:, 0:1], in0=btm[:], in1=bcondf[:],
                             op=OP.mult)
            DV.tensor_tensor(out=bt4[:, 1:2], in0=blo[a][:], in1=bcondn[:],
                             op=OP.mult)
            DV.tensor_tensor(out=bt4[:, 2:3], in0=bhi[a][:], in1=bcondf[:],
                             op=OP.mult)
            DV.tensor_tensor(out=bt4[:, 3:4], in0=btm[:], in1=bcondn[:],
                             op=OP.mult)
            DV.drain()
            DV.tensor_tensor(out=blo[b][:], in0=bt4[:, 0:1], in1=bt4[:, 1:2],
                             op=OP.add)
            DV.tensor_tensor(out=bhi[b][:], in0=bt4[:, 2:3], in1=bt4[:, 3:4],
                             op=OP.add)
            DV.drain()

        def emit_bisect_final():
            fin = BITERS % 2
            DV.tensor_scalar(out=mask16[:], in0=normsT[:], scalar1=bhi[fin][:],
                             scalar2=None, op0=OP.is_gt)
            s_vv.inc(DV.drain())

        sums_hT = shscr[0:16, :]
        scale_hT = mask16

        DV.wait_ge(s_gd.sem, bva_gd)
        wv_done = emit_stage(wvT, wbfC, cast_vaug,
                             hook=lambda q: emit_bisect_iter(q))
        DV.wait_ge(s_vv.sem, wv_done)
        for c in range(NCH):
            zv = wbfC[:, VAW * c:VAW * (c + 1)].rearrange(
                "p (h w) -> p h w", h=H)[:, :, 64:65]
            DV.memset(zv, 0.0)
        s_vv.inc(DV.drain())
        vaug_ready = s_vv.v

        with contextlib.ExitStack() as esP:
            p_vs = [esP.enter_context(nc.psum_tensor(f"p_v{i}", [128, VAW], F32))
                    for i in range(2)]
            PE.wait_ge(s_vv.sem, max(vaug_ready, kmax_ready))
            vv_b = s_vv.v
            vmm = []
            for i in range(NTT):
                pv = p_vs[i % 2]
                if i >= 2:
                    PE.wait_ge(s_vv.sem, vv_b + (i - 2) + 1)
                for c in range(NCH):
                    for (n0, n1) in [(0, 512), (512, 1024), (1024, VAW)]:
                        PE.matmul(pv[:, n0:n1],
                                  xn[:, L * c + 128 * i:L * c + 128 * (i + 1)],
                                  wbfC[:, VAW * c + n0:VAW * c + n1],
                                  start=(c == 0), stop=False)
                for (n0, n1) in [(0, 512), (512, 1024), (1024, VAW)]:
                    mm = PE.matmul(pv[:, n0:n1], ONESBF[:, 0:128],
                                   bva[:, n0:n1], start=False, stop=True)
                s_mm.inc(mm)
                vmm.append(s_mm.v)
            for i in range(NTT):
                DV.wait_ge(s_mm.sem, vmm[i])
                DV.tensor_copy(out=vt[:, VAW * i:VAW * (i + 1)],
                               in_=p_vs[i % 2][:])
                s_vv.inc(DV.drain())
                if 8 + i < BITERS:
                    emit_bisect_iter(8 + i)
            for _it in range(8 + NTT, BITERS):
                emit_bisect_iter(_it)
            emit_bisect_final()
            if stop == 35:
                dbg_dump(mask16[:], 16, L, [(s_vv.sem, s_vv.v)])

        AC.wait_ge(s_vv.sem, s_vv.v)
        AC.activation(out=sqq[:], in_=sqq[:], func=AF.Sqrt)
        AC.drain()
        AC.activation(out=sqk[:], in_=kmaxv[:], func=AF.Sqrt)
        s_aa.inc(AC.drain())
        DV.wait_ge(s_aa.sem, s_aa.v)
        DV.tensor_tensor(out=negB[:], in0=sqq[:], in1=sqk[:], op=OP.mult)
        DV.drain()
        DV.tensor_scalar(out=negB[:], in0=negB[:], scalar1=-SCALE, scalar2=None,
                         op0=OP.mult)
        s_vv.inc(DV.drain())
        with contextlib.ExitStack() as esP:
            p_bnd = esP.enter_context(nc.psum_tensor("p_bnd", [128, 512], F32))
            PE.wait_ge(s_vv.sem, s_vv.v)
            mm = PE.transpose(p_bnd[0:1, 0:16], negB[:], IDT[0:16, 0:16])
            s_mm.inc(mm)
            DV.wait_ge(s_mm.sem, s_mm.v)
            DV.tensor_copy(out=negBr[:], in_=p_bnd[0:1, 0:16])
            s_vv.inc(DV.drain())
            PE.wait_ge(s_vv.sem, s_vv.v)
            mm = PE.matmul(p_bnd[:, 0:16], ONES[64:65, :], negBr[:],
                           start=True, stop=True)
            s_mm.inc(mm)
            DV.wait_ge(s_mm.sem, s_mm.v)
            DV.tensor_copy(out=negBb[:], in_=p_bnd[:, 0:16])
            s_vv.inc(DV.drain())
        bound_ready = s_vv.v
        if stop == 4:
            dbg_dump(vt[:, 0:4096], 128, 4096, [(s_vv.sem, s_vv.v)])
        if stop == 5:
            dbg_dump(negBb[:], 128, 16, [(s_vv.sem, bound_ready)])

        wfv = wfT.rearrange("(c p) o -> p c o", p=128)
        f1_ld = {}
        f1_abs = {}
        f1_mm = {}

        wo_done = emit_stage(woT, wbfA, cast_plain,
                             gate=(s_mm.sem, q_mm_all))

        def emit_f1_chunk(c):
            if c == 0:
                SY.wait_ge(s_vv.sem, wo_done)
            else:
                SY.wait_ge(s_aa.sem, f1_abs[c - 1])
                SY.wait_ge(s_mm.sem, f1_mm[c - 1])
            s_ld.inc(SY.dma_start(out=wstage[:], in_=wfv[:, c, :]))
            f1_ld[c] = s_ld.v
            AC.wait_ge(s_ld.sem, f1_ld[c])
            s_aa.inc(AC.activation(out=wstage[:], in_=wstage[:], func=AF.Abs))
            f1_abs[c] = s_aa.v
            PE.wait_ge(s_aa.sem, f1_abs[c])
            for (n0, n1) in [(0, 512), (512, 1024)]:
                mm = PE.matmul(p_ws[0:1, n0:n1], ONES[:, 0:1],
                               wstage[:, n0:n1],
                               start=(c == 0), stop=(c == NCH - 1))
            s_mm.inc(mm)
            f1_mm[c] = s_mm.v

        expT = xn
        esWS = contextlib.ExitStack()
        p_ws = esWS.enter_context(nc.psum_tensor("p_ws", [1, 1024], F32))
        with contextlib.ExitStack() as esAT:
            p_sc = [esAT.enter_context(nc.psum_tensor(f"p_sc{i}", [128, 512], F32))
                    for i in range(4)]
            p_av = [esAT.enter_context(nc.psum_tensor(f"p_av{i}", [VW, 512], F32))
                    for i in range(2)]

            def attnC_ap(pair):
                reg = wbfB if pair < 4 else wbfC
                return reg[:, L * (pair % 4):L * (pair % 4 + 1)]

            PE.wait_ge(s_vv.sem, bound_ready)
            exp_aa = {}
            av_mm = {}
            av_vv = {}
            NIT = H * 4

            def emit_scores(it):
                h, s = it // 4, it % 4
                pair, r0 = h // 2, 64 * (h % 2)
                ebuf = expT[:, 8192 * (it % 2):8192 * (it % 2 + 1)]
                for j in range(NTT):
                    tile = j % 4
                    if j >= 4:
                        PE.wait_ge(s_aa.sem, exp_aa[(it, j - 4)])
                    elif it >= 1:
                        PE.wait_ge(s_aa.sem, exp_aa[(it - 1, 12 + j)])
                    mm = PE.matmul(
                        p_sc[tile][:],
                        kf[r0:r0 + 64,
                           L * pair + 128 * j:L * pair + 128 * (j + 1)],
                        qf[r0:r0 + 64,
                           L * pair + 512 * s:L * pair + 512 * (s + 1)],
                        start=True, stop=True)
                    s_mm.inc(mm)
                    AC.wait_ge(s_mm.sem, s_mm.v)
                    if it >= 2 and j == 0:
                        AC.wait_ge(s_mm.sem, av_mm[it - 2])
                    s_aa.inc(AC.activation(
                        out=ebuf[:, 512 * j:512 * (j + 1)], in_=p_sc[tile][:],
                        func=AF.Exp, scale=SCALE, bias=negBb[:, h:h + 1]))
                    exp_aa[(it, j)] = s_aa.v

            def emit_av(it):
                h, s = it // 4, it % 4
                pair, r0 = h // 2, 64 * (h % 2)
                ebuf = expT[:, 8192 * (it % 2):8192 * (it % 2 + 1)]
                if it >= 2:
                    PE.wait_ge(s_vv.sem, av_vv[it - 2])
                for j in range(NTT):
                    PE.wait_ge(s_aa.sem, exp_aa[(it, j)])
                    mm = PE.matmul(p_av[it % 2][:],
                                   vt[:, VAW * j + VW * h:VAW * j + VW * (h + 1)],
                                   ebuf[:, 512 * j:512 * (j + 1)],
                                   start=(j == 0), stop=(j == NTT - 1))
                s_mm.inc(mm)
                av_mm[it] = s_mm.v
                srb = shscr[32:33, 512 * (it % 2):512 * (it % 2 + 1)]
                DV.wait_ge(s_mm.sem, s_mm.v)
                if it >= 2:
                    DV.wait_ge(s_gs[it % 2].sem, 16 * (it // 2))
                DV.tensor_copy(out=srb, in_=p_av[it % 2][64:65, :])
                s_vv.inc(DV.tensor_copy(
                    out=attnC_ap(pair)[r0:r0 + 64, 512 * s:512 * (s + 1)],
                    in_=p_av[it % 2][0:64, :]))
                av_vv[it] = s_vv.v
                GP.wait_ge(s_vv.sem, s_vv.v)
                s_gs[it % 2].inc(GP.dma_start(
                    out=sums_hT[h:h + 1, 512 * s:512 * (s + 1)], in_=srb))

            for it in range(NIT):
                emit_scores(it)
                if it >= 1:
                    emit_av(it - 1)
                if 2 <= it < 2 + NCH:
                    emit_f1_chunk(it - 2)
            emit_av(NIT - 1)

        with contextlib.ExitStack() as esP:
            p_m = esP.enter_context(nc.psum_tensor("p_m2", [128, 512], F32))
            DV.wait_ge(s_mm.sem, f1_mm[NCH - 1])
            DV.tensor_reduce(out=wsum[:], in_=p_ws[0:1, :], axis=AX.X,
                             op=OP.add)
            DV.drain()
            DV.tensor_scalar(out=thr1[:], in0=wsum[:], scalar1=0.5 / (D * D),
                             scalar2=None, op0=OP.mult)
            DV.drain()
            DV.tensor_scalar(out=wsum[:], in0=wsum[:], scalar1=1.0 / (D * D),
                             scalar2=None, op0=OP.mult)
            s_vv.inc(DV.drain())
            PE.wait_ge(s_vv.sem, s_vv.v)
            PE.matmul(p_m[:, 1:2], ONES[0:1, :], thr1[0:1, 0:1], start=True,
                      stop=True)
            mm = PE.matmul(p_m[:, 2:3], ONES[0:1, :], wsum[0:1, 0:1],
                           start=True, stop=True)
            s_mm.inc(mm)
            DV.wait_ge(s_mm.sem, s_mm.v)
            DV.tensor_copy(out=thrb[:], in_=p_m[:, 1:2])
            DV.tensor_copy(out=wscb[:], in_=p_m[:, 2:3])
            DV.tensor_scalar(out=nthrb[:], in0=p_m[:, 1:2], scalar1=-1.0,
                             scalar2=None, op0=OP.mult)
            s_vv.inc(DV.drain())
            thr_ready = s_vv.v
        esWS.close()

        if stop == 6:
            GP.wait_ge(s_vv.sem, s_vv.v)
            dbg_dump(wbfB[:, 0:4096], 128, 4096, [(s_gd.sem, s_gd.v)])
        if stop == 65:
            dbg_dump(sums_hT[:], 16, L, [(s_gd.sem, s_gd.v)])
        DV.wait_ge(s_gs[0].sem, s_gs[0].v)
        DV.wait_ge(s_gs[1].sem, s_gs[1].v)
        DV.reciprocal(out=sums_hT[:], in_=sums_hT[:])
        DV.drain()
        DV.tensor_tensor(out=scale_hT[:], in0=sums_hT[:], in1=mask16[:],
                         op=OP.mult)
        s_vv.inc(DV.drain())
        with contextlib.ExitStack() as esP:
            p_scl = [esP.enter_context(nc.psum_tensor(f"p_scl{i}", [128, 512],
                                                      F32)) for i in range(2)]
            PE.wait_ge(s_vv.sem, s_vv.v)
            scl_vv = []
            for i, (pair, s) in enumerate((p, s) for p in range(NCH)
                                          for s in range(4)):
                if i >= 2:
                    PE.wait_ge(s_vv.sem, scl_vv[i - 2])
                mm = PE.matmul(p_scl[i % 2][:],
                               cE[:, 128 * pair:128 * (pair + 1)],
                               scale_hT[:, 512 * s:512 * (s + 1)],
                               start=True, stop=True)
                s_mm.inc(mm)
                DV.wait_ge(s_mm.sem, s_mm.v)
                s_vv.inc(DV.tensor_tensor(
                    out=attnC_ap(pair)[:, 512 * s:512 * (s + 1)],
                    in0=attnC_ap(pair)[:, 512 * s:512 * (s + 1)],
                    in1=p_scl[i % 2][:], op=OP.mult))
                scl_vv.append(s_vv.v)
            PE.wait_ge(s_vv.sem, s_vv.v)
        attn_done = s_vv.v
        if stop == 7:
            dbg_dump(wbfB[:, 0:4096], 128, 4096, [(s_vv.sem, attn_done)])

    scope1.close()

    with contextlib.ExitStack() as esC:
        x1 = esC.enter_context(nc.sbuf_tensor("x1", [128, NCH * L], F32))
        xrld = [esC.enter_context(nc.sbuf_tensor(f"xrld{i}", [128, 512], F32))
                for i in range(2)]
        x2st = esC.enter_context(nc.sbuf_tensor("x2st", [128, L], F32))
        rstd2 = esC.enter_context(nc.sbuf_tensor("rstd2", [128, L], F32))
        mr2 = esC.enter_context(nc.sbuf_tensor("mr2", [128, L], F32))
        qbf = esC.enter_context(nc.sbuf_tensor("qbf", [128, NCH * L], BF16))
        sfb = esC.enter_context(nc.sbuf_tensor("sfb", [1, D], F32))
        badjb = esC.enter_context(nc.sbuf_tensor("badjb", [1, D], BF16))
        lmax8 = esC.enter_context(nc.sbuf_tensor("lmax8", [128, NCH], F32))
        lmax1 = esC.enter_context(nc.sbuf_tensor("lmax1", [128, 1], F32))
        lmaxr = esC.enter_context(nc.sbuf_tensor("lmaxr", [1, 128], F32))
        cmax = esC.enter_context(nc.sbuf_tensor("cmax", [128, 1], F32))
        gmaxv = esC.enter_context(nc.sbuf_tensor("gmaxv", [128, 1], F32))
        recipb = esC.enter_context(nc.sbuf_tensor("recipb", [128, 1], F32))
        wsisb = esC.enter_context(nc.sbuf_tensor("wsisb", [128, 1], F32))
        isv = esC.enter_context(nc.sbuf_tensor("isv", [128, 1], F32))
        wsis1 = esC.enter_context(nc.sbuf_tensor("wsis1", [1, 1], F32))
        rws = esC.enter_context(nc.sbuf_tensor("rws", [1, 1], F32))
        wabs = x2st[:, 0:D]
        rr_b = rstd2
        mrr_b = mr2

        SY.wait_ge(s_vv.sem, attn_done)
        s_lf.inc(SY.dma_start(out=sfb[:], in_=fb_r))
        sfb_ld = s_lf.v

        def attnC_ap(pair):
            reg = wbfB if pair < 4 else wbfC
            return reg[:, L * (pair % 4):L * (pair % 4 + 1)]

        with contextlib.ExitStack() as esP:
            p_w = [esP.enter_context(nc.psum_tensor(f"p_w{i}", [128, 512], F32))
                   for i in range(2)]
            xv = xT.rearrange("(c p) t -> p c t", p=128)
            pass
            vv_b = s_vv.v
            for i, (o, t) in enumerate(GROUPS):
                SY.wait_ge(s_vv.sem, vv_b if i < 2 else vv_b + (i - 2) + 1)
                s_lr[i % 2].inc(SY.dma_start(
                    out=xrld[i % 2][:],
                    in_=xv[:, o, 512 * t:512 * (t + 1)]))
            PE.wait_ge(s_vv.sem, max(attn_done, wo_done))
            wo_mm = []
            for i, (o, t) in enumerate(GROUPS):
                if i >= 2:
                    PE.wait_ge(s_vv.sem, vv_b + (i - 2) + 1)
                for c in range(NCH):
                    mm = PE.matmul(
                        p_w[i % 2][:],
                        wbfA[:, D * c + 128 * o:D * c + 128 * (o + 1)],
                        attnC_ap(c)[:, 512 * t:512 * (t + 1)],
                        start=(c == 0), stop=(c == NCH - 1))
                s_mm.inc(mm)
                wo_mm.append(s_mm.v)
            for i, (o, t) in enumerate(GROUPS):
                DV.wait_ge(s_mm.sem, wo_mm[i])
                DV.wait_ge(s_lr[i % 2].sem, 16 * (i // 2 + 1))
                s_vv.inc(DV.scalar_tensor_tensor(
                    out=x1[:, L * o + 512 * t:L * o + 512 * (t + 1)],
                    in0=p_w[i % 2][:], scalar=sbo[:, o:o + 1],
                    in1=xrld[i % 2][:], op0=OP.add, op1=OP.add))
            x1_done = s_vv.v
            wo_mm_all = s_mm.v
            if stop == 8:
                dbg_dump(x1[:, 0:4096], 128, 4096, [(s_vv.sem, x1_done)])

        with contextlib.ExitStack() as esLP:
            p_rb, p_mb, ln2_rdy = emit_ln(
                lambda c: x1[:, L * c:L * (c + 1)], x1_done, x2st,
                qbf[:, 0:L], esLP)
            DV.wait_ge(s_vv.sem, ln2_rdy)
            DV.tensor_copy(out=rstd2[:], in_=p_rb[:])
            s_vv.inc(DV.tensor_copy(out=mr2[:], in_=p_mb[:]))
            DV.drain()
            rstd2_done = s_vv.v

        for c in range(NCH):
            DV.tensor_tensor(out=x2st[:], in0=x1[:, L * c:L * (c + 1)],
                             in1=rstd2[:], op=OP.mult)
            DV.drain()
            DV.tensor_tensor(out=x2st[:], in0=x2st[:], in1=mr2[:],
                             op=OP.subtract)
            DV.drain()
            DV.tensor_reduce(out=lmax8[:, c:c + 1], in_=x2st[:], axis=AX.X,
                             op=OP.max, apply_absolute_value=True)
            DV.drain()
        DV.tensor_reduce(out=lmax1[:], in_=lmax8[:], axis=AX.X, op=OP.max)
        s_vv.inc(DV.drain())
        with contextlib.ExitStack() as esP:
            p_m = esP.enter_context(nc.psum_tensor("p_m1", [128, 512], F32))
            PE.wait_ge(s_vv.sem, s_vv.v)
            mm = PE.transpose(p_m[0:1, 0:128], lmax1[:], IDT[:, 0:128])
            s_mm.inc(mm)
            DV.wait_ge(s_mm.sem, s_mm.v)
            DV.tensor_copy(out=lmaxr[:], in_=p_m[0:1, 0:128])
            DV.drain()
            DV.tensor_reduce(out=wsum[:], in_=lmaxr[:], axis=AX.X, op=OP.max)
            s_vv.inc(DV.drain())
            PE.wait_ge(s_vv.sem, s_vv.v)
            mm = PE.matmul(p_m[:, 0:1], ONES[0:1, :], wsum[0:1, 0:1],
                           start=True, stop=True)
            s_mm.inc(mm)
            DV.wait_ge(s_mm.sem, s_mm.v)
            DV.tensor_copy(out=cmax[:], in_=p_m[:, 0:1])
            s_vv.inc(DV.drain())
            cmax_done = s_vv.v

        GP.wait_ge(s_vv.sem, cmax_done)
        s_gd.inc(GP.dma_start(out=cc_in[:, :], in_=cmax[:]))
        GP.wait_ge(s_gd.sem, s_gd.v)
        GP.collective_compute(
            "AllReduce", OP.max, replica_groups=[list(range(NCORES))],
            ins=[cc_in.ap().opt()], outs=[cc_out.ap().opt()],
        ).then_inc(s_cc.sem, 1)
        s_cc.v += 1
        GP.wait_ge(s_cc.sem, s_cc.v)
        s_gd.inc(GP.dma_start(out=gmaxv[:], in_=cc_out[:, :]))
        gmax_gd = s_gd.v

        vvp0 = s_vv.v
        for c in range(NCH):
            SY.wait_ge(s_vv.sem, s_vv.v if c == 0 else vvp0 + c)
            s_ld.inc(SY.dma_start(out=wstage[:], in_=wfv[:, c, :]))
            DV.wait_ge(s_ld.sem, s_ld.v)
            if c == 0:
                DV.wait_ge(s_vv.sem, thr_ready)
                DV.wait_ge(s_mm.sem, wo_mm_all)
            DV.tensor_scalar(out=x2st[:, 0:D], in0=wstage[:], scalar1=thrb[:],
                             scalar2=None, op0=OP.is_gt)
            DV.tensor_scalar(out=x2st[:, D:2 * D], in0=wstage[:],
                             scalar1=nthrb[:], scalar2=None, op0=OP.is_lt)
            DV.drain()
            DV.tensor_tensor(out=wbfA[:, D * c:D * (c + 1)],
                             in0=x2st[:, 0:D], in1=x2st[:, D:2 * D],
                             op=OP.subtract)
            s_vv.inc(DV.drain())
        qwt_done = s_vv.v

        DV.wait_ge(s_gd.sem, gmax_gd)
        DV.wait_ge(s_lf.sem, sfb_ld)
        DV.reciprocal(out=recipb[:], in_=gmaxv[:])
        DV.drain()
        DV.tensor_scalar(out=recipb[:], in0=recipb[:], scalar1=127.0,
                         scalar2=None, op0=OP.mult)
        DV.tensor_scalar(out=isv[:], in0=gmaxv[:], scalar1=1.0 / 127.0,
                         scalar2=None, op0=OP.mult)
        DV.drain()
        DV.tensor_tensor(out=wsisb[:], in0=isv[:], in1=wscb[:], op=OP.mult)
        DV.drain()
        DV.tensor_copy(out=wsis1[:], in_=wsisb[0:1, 0:1])
        DV.drain()
        DV.reciprocal(out=rws[:], in_=wsis1[:])
        DV.drain()
        DV.tensor_scalar(out=badjb[:], in0=sfb[:], scalar1=rws[:], scalar2=None,
                         op0=OP.mult)
        DV.drain()
        DV.tensor_scalar(out=rr_b[:], in0=rstd2[:], scalar1=recipb[:],
                         scalar2=None, op0=OP.mult)
        DV.drain()
        DV.tensor_scalar(out=mrr_b[:], in0=mr2[:], scalar1=recipb[:],
                         scalar2=None, op0=OP.mult)
        s_vv.inc(DV.drain())

        if stop == 85:
            dbg_dump(badjb[:], 1, 1024, [(s_vv.sem, s_vv.v)])
        if stop == 86:
            dbg_dump(sfb[:], 1, 1024, [(s_vv.sem, s_vv.v)])
        qi32 = esC.enter_context(nc.sbuf_tensor("qi32", [128, L], I32))
        x2stB = esC.enter_context(nc.sbuf_tensor("x2stB", [128, L], F32))
        aa_qz = s_aa.v
        for c in range(NCH):
            if c >= 1:
                DV.wait_ge(s_aa.sem, aa_qz + c)
            DV.tensor_tensor(out=x2st[:], in0=x1[:, L * c:L * (c + 1)],
                             in1=rr_b[:], op=OP.mult)
            DV.drain()
            DV.tensor_tensor(out=qi32[:], in0=x2st[:], in1=mrr_b[:],
                             op=OP.subtract)
            s_vv.inc(DV.drain())
            AC.wait_ge(s_vv.sem, s_vv.v)
            s_aa.inc(AC.copy(out=qbf[:, L * c:L * (c + 1)], in_=qi32[:]))
        quant_done = s_aa.v
        if stop == 9:
            dbg_dump(qbf[:, 0:4096], 128, 4096, [(s_aa.sem, quant_done)])
        if stop == 95:
            dbg_dump(gmaxv[:], 128, 1, [(s_aa.sem, quant_done)])

        with contextlib.ExitStack() as esP:
            p_f = [esP.enter_context(nc.psum_tensor(f"p_f{i}", [128, 512],
                                                    F32)) for i in range(8)]
            PE.wait_ge(s_vv.sem, qwt_done)
            yv = yT.rearrange("(c p) t -> p c t", p=128)
            st_base = s_st.v
            evac_last = {}
            for g in range(4):
                for c in range(NCH):
                    PE.wait_ge(s_aa.sem, aa_qz + c + 1)
                    for k, oo in enumerate((2 * g, 2 * g + 1)):
                        for t in range(4):
                            if g >= 1 and c == 0:
                                PE.wait_ge(s_vv.sem,
                                           evac_last[g - 1] - 7 + 4 * k + t)
                            PE.matmul(
                                p_f[4 * k + t][:],
                                wbfA[:, D * c + 128 * oo:D * c + 128 * (oo + 1)],
                                qbf[:, L * c + 512 * t:L * c + 512 * (t + 1)],
                                start=(c == 0), stop=False)
                for k, oo in enumerate((2 * g, 2 * g + 1)):
                    for t in range(4):
                        mm = PE.matmul(p_f[4 * k + t][:],
                                       badjb[:, 128 * oo:128 * (oo + 1)],
                                       ONESBF[:, 0:512], start=False,
                                       stop=True)
                        s_mm.inc(mm)
                mm_done = s_mm.v
                xbuf = [x2st, x2stB]
                for k, oo in enumerate((2 * g, 2 * g + 1)):
                    for t in range(4):
                        DV.wait_ge(s_mm.sem, mm_done - 7 + 4 * k + t)
                        if g >= 1 and k == 0 and t == 0:
                            DV.wait_ge(s_st.sem, st_base + 32 * g)
                        s_vv.inc(DV.scalar_tensor_tensor(
                            out=xbuf[k][:, 512 * t:512 * (t + 1)],
                            in0=p_f[4 * k + t][:], scalar=wsisb[:],
                            in1=x1[:, L * oo + 512 * t:L * oo + 512 * (t + 1)],
                            op0=OP.mult, op1=OP.add))
                evac_last[g] = s_vv.v
                for k, oo in enumerate((2 * g, 2 * g + 1)):
                    SY.wait_ge(s_vv.sem, evac_last[g] - 4 * (1 - k))
                    s_st.inc(SY.dma_start(out=yv[:, oo, :], in_=xbuf[k][:]))
            SY.wait_ge(s_st.sem, s_st.v)

    es.close()
    return nc


def _make_consts():
    import ml_dtypes
    c32 = np.zeros((128, 392), np.float32)
    c32[:, 0:128] = 1.0
    for c in range(NCH):
        for h in range(16):
            if h // 2 == c:
                ps = 64 * (h % 2)
                c32[ps:ps + 64, 128 + 16 * c + h] = 1.0
    c32[:, 256:384] = np.eye(128, dtype=np.float32)
    c32[:, 385] = 1e-5
    c32[:, 386] = -0.5
    c32[:, 387] = SCALE
    cE = np.zeros((16, 1024), np.float32)
    for c in range(NCH):
        for h in range(16):
            if h // 2 == c:
                ps = 64 * (h % 2)
                cE[h, 128 * c + ps:128 * c + ps + 64] = 1.0
    return c32, cE.astype(ml_dtypes.bfloat16)


_NC_CACHE = {}


def kernel(L0_attn_x, L0_attn_Wq, L0_attn_bq, L0_attn_Wk, L0_attn_bk,
           L0_attn_Wv, L0_attn_bv, L0_attn_Wo, L0_attn_bo,
           L0_ffn_weight, L0_ffn_bias):
    x = np.asarray(L0_attn_x, np.float32)
    c32, cE = _make_consts()

    def tr(w):
        return np.ascontiguousarray(np.asarray(w, np.float32).T)

    def cols(b):
        return np.ascontiguousarray(
            np.asarray(b, np.float32).reshape(NCH, 128).T)

    common = {
        "wqT": tr(L0_attn_Wq), "wkT": tr(L0_attn_Wk), "wvT": tr(L0_attn_Wv),
        "woT": tr(L0_attn_Wo), "wfT": tr(L0_ffn_weight),
        "bq_c": cols(L0_attn_bq), "bk_c": cols(L0_attn_bk),
        "bo_c": cols(L0_attn_bo),
        "bv_r": np.asarray(L0_attn_bv, np.float32).reshape(1, D),
        "fb_r": np.asarray(L0_ffn_bias, np.float32).reshape(1, D),
        "cst32": c32, "cstE": cE,
    }
    in_maps = []
    for b in range(NCORES):
        m = dict(common)
        m["xT"] = np.ascontiguousarray(x[b].T)
        in_maps.append(m)

    import os
    stop = int(os.environ.get("KSTOP", "99"))
    if "nc" not in _NC_CACHE:
        _NC_CACHE["nc"] = build_nc(stop)
    nc = _NC_CACHE["nc"]
    res = run_bass_kernel_spmd(nc, in_maps, core_ids=list(range(NCORES)))
    out = np.stack([np.ascontiguousarray(res.results[b]["yT"].T)
                    for b in range(NCORES)], axis=0)
    return out


def build_nc(stop=99):
    box = {}
    try:
        return _build_nc_inner(stop, box)
    except Exception as e:
        if type(e).__name__ == "StopEmit":
            return box["nc"]
        raise


# version 29
# speedup vs baseline: 1.0060x; 1.0060x over previous
import contextlib
import numpy as np

import concourse.bass as bass
import concourse.mybir as mybir
from concourse.bass_utils import run_bass_kernel_spmd

F32 = mybir.dt.float32
BF16 = mybir.dt.bfloat16
I32 = mybir.dt.int32
AF = mybir.ActivationFunctionType
OP = mybir.AluOpType
AX = mybir.AxisListType

D = 1024
L = 2048
H = 16
NCH = D // 128
NTT = L // 128
VW = 65
VAW = H * VW
SCALE = 0.125
NCORES = 8
BITERS = 22
GROUPS = [(o, t) for o in range(NCH) for t in range(4)]


class Ctr:
    def __init__(self, sem, step=1):
        self.sem = sem
        self.v = 0
        self.step = step

    def inc(self, instr, n=1):
        instr.then_inc(self.sem, n * self.step)
        self.v += n * self.step
        return instr


def _build_nc_inner(stop=99, box=None):
    nc = bass.Bass("TRN2", target_bir_lowering=False)
    PE, DV, AC, GP, SY = nc.tensor, nc.vector, nc.scalar, nc.gpsimd, nc.sync

    class StopEmit(Exception):
        pass

    xT = nc.dram_tensor("xT", [D, L], F32, kind="ExternalInput").ap()
    wqT = nc.dram_tensor("wqT", [D, D], F32, kind="ExternalInput").ap()
    wkT = nc.dram_tensor("wkT", [D, D], F32, kind="ExternalInput").ap()
    wvT = nc.dram_tensor("wvT", [D, D], F32, kind="ExternalInput").ap()
    woT = nc.dram_tensor("woT", [D, D], F32, kind="ExternalInput").ap()
    wfT = nc.dram_tensor("wfT", [D, D], F32, kind="ExternalInput").ap()
    bq_c = nc.dram_tensor("bq_c", [128, NCH], F32, kind="ExternalInput").ap()
    bk_c = nc.dram_tensor("bk_c", [128, NCH], F32, kind="ExternalInput").ap()
    bo_c = nc.dram_tensor("bo_c", [128, NCH], F32, kind="ExternalInput").ap()
    bv_r = nc.dram_tensor("bv_r", [1, D], F32, kind="ExternalInput").ap()
    fb_r = nc.dram_tensor("fb_r", [1, D], F32, kind="ExternalInput").ap()
    cst32 = nc.dram_tensor("cst32", [128, 392], F32, kind="ExternalInput").ap()
    cstE = nc.dram_tensor("cstE", [16, 1024], BF16, kind="ExternalInput").ap()
    yT = nc.dram_tensor("yT", [D, L], F32, kind="ExternalOutput").ap()
    cc_in = nc.dram_tensor("cc_in", [128, 1], F32)
    cc_out = nc.dram_tensor("cc_out", [128, 1], F32)

    es = contextlib.ExitStack()
    if box is not None:
        box["nc"] = nc
        box["es"] = es
    SB = lambda n, s, d: es.enter_context(nc.sbuf_tensor(n, s, d))

    def SEM(name, step=1):
        return Ctr(es.enter_context(nc.semaphore(name)), step)

    s_ld = SEM("s_ld", 16)
    s_gd = SEM("s_gd", 16)
    s_mm = SEM("s_mm")
    s_vv = SEM("s_vv")
    s_aa = SEM("s_aa")
    s_cc = SEM("s_cc")
    s_st = SEM("s_st", 16)
    s_lx = [SEM(f"s_lx{i}", 16) for i in range(NCH)]
    s_lr = [SEM("s_lr0", 16), SEM("s_lr1", 16)]
    s_lf = SEM("s_lf", 16)

    def dbg_dump(ap, rows, cols, gate_sems=()):
        for sem, val in gate_sems:
            GP.wait_ge(sem, val)
        ytf = yT.rearrange("a b -> (a b)").rearrange("(r c) -> r c", c=cols)
        s_gd.inc(GP.dma_start(out=ytf[0:rows, :], in_=ap))
        GP.wait_ge(s_gd.sem, s_gd.v)
        err = StopEmit()
        err.nc = None
        raise err

    c32 = SB("c32", [128, 392], F32)
    nc.const_aps.register(F32, 0.0, c32[:, 384:385])
    nc.const_aps.register(F32, 1e-5, c32[:, 385:386])
    nc.const_aps.register(F32, -0.5, c32[:, 386:387])
    nc.const_aps.register(F32, SCALE, c32[:, 387:388])
    nc.const_aps.register(F32, 1.0, c32[:, 0:1])
    ONES = c32[:, 0:128]
    B16 = c32[:, 128:256]
    IDT = c32[:, 256:384]
    cE = SB("cE", [16, 1024], BF16)
    bfpool = SB("bfpool", [128, 1552], BF16)
    bva = bfpool[0:1, 0:VAW]
    ONESBF = bfpool[0:1, VAW:VAW + 512]
    sbq = SB("sbq", [128, NCH], F32)
    sbk = SB("sbk", [128, NCH], F32)
    sbo = SB("sbo", [128, NCH], F32)

    kmaxv = SB("kmaxv", [16, 1], F32)
    sqq = SB("sqq", [16, 1], F32)
    sqk = SB("sqk", [16, 1], F32)
    negB = SB("negB", [16, 1], F32)
    thrb = c32[:, 388:389]
    wscb = c32[:, 389:390]
    wsum = c32[0:1, 390:391]
    thr1 = c32[0:1, 391:392]
    negBb = SB("negBb", [128, 16], F32)
    nthrb = negBb[:, 0:1]
    blo = [negBb[0:16, 4:5], negBb[0:16, 5:6]]
    bhi = [negBb[0:16, 6:7], negBb[0:16, 7:8]]
    btm = negBb[0:16, 8:9]
    bcount = negBb[0:16, 9:10]
    bcondf = negBb[0:16, 10:11]
    bcondn = negBb[0:16, 11:12]
    bt4 = negBb[0:16, 12:16]

    wbfA = SB("wbfA", [128, NCH * D], BF16)
    wbfB = SB("wbfB", [128, NCH * D], BF16)
    wbfC = SB("wbfC", [128, NCH * VAW], BF16)
    wstage = SB("wstage", [128, D], F32)

    for dst, srcd in [(c32, cst32), (cE, cstE), (sbq, bq_c),
                      (sbk, bk_c), (sbo, bo_c)]:
        s_ld.inc(SY.dma_start(out=dst[:], in_=srcd))
    cst_ld = s_ld.v
    SY.wait_ge(s_ld.sem, cst_ld)
    DV.wait_ge(s_ld.sem, cst_ld)
    DV.memset(bva[:], 1.0)
    DV.memset(ONESBF, 1.0)
    s_vv.inc(DV.drain())
    GP.wait_ge(s_vv.sem, s_vv.v)
    s_gd.inc(GP.dma_start(
        out=bva.rearrange("a (h w) -> a h w", h=H)[:, :, 0:64],
        in_=bv_r.rearrange("a (h w) -> a h w", h=H)))
    bva_gd = s_gd.v

    def cast_plain(dst_bf, c):
        return DV.tensor_copy(out=dst_bf[:, D * c:D * (c + 1)], in_=wstage[:])

    def cast_vaug(dst_bf, c):
        dv = dst_bf[:, VAW * c:VAW * (c + 1)].rearrange(
            "p (h w) -> p h w", h=H)[:, :, 0:64]
        return DV.tensor_copy(
            out=dv, in_=wstage[:].rearrange("p (h w) -> p h w", h=H))

    def emit_stage(dram, dst, castfn, gate=None, hook=None):
        wv_ = dram.rearrange("(c p) o -> p c o", p=128)
        for q in range(NCH):
            SY.wait_ge(s_vv.sem, s_vv.v)
            s_ld.inc(SY.dma_start(out=wstage[:], in_=wv_[:, q, :]))
            DV.wait_ge(s_ld.sem, s_ld.v)
            if gate is not None and q == 0:
                DV.wait_ge(gate[0], gate[1])
            s_vv.inc(castfn(dst, q))
            if hook is not None:
                hook(q)
        return s_vv.v

    _ln_tag = [0]

    def emit_ln(src_of, gate_vv, scr, xnb, psum_scope, x_gate=None):
        _ln_tag[0] += 1
        tag = str(_ln_tag[0])
        """Computes rstd_b/mr_b [128, L] psum tiles for LN(src).
        src_of(c): [128, L] f32 slice. scr: [128, L] f32 scratch.
        xnb: [128, L] bf16 scratch. psum_scope: ExitStack owning the
        returned p_rb/p_mb tiles. Returns (p_rb, p_mb, vv_ready)."""
        aa0 = s_aa.v
        mm0 = s_mm.v
        with contextlib.ExitStack() as esP:
            p_s = esP.enter_context(nc.psum_tensor("p_s" + tag, [1, L], F32))
            p_q = esP.enter_context(nc.psum_tensor("p_q" + tag, [1, L], F32))
            for c in range(NCH):
                if x_gate is not None:
                    x_gate(c, AC)
                else:
                    AC.wait_ge(s_vv.sem, gate_vv)
                if c >= 1:
                    AC.wait_ge(s_mm.sem, mm0 + c)
                s_aa.inc(AC.activation(out=scr[:], in_=src_of(c),
                                       func=AF.Square))
            for c in range(NCH):
                if x_gate is not None:
                    x_gate(c, PE)
                else:
                    PE.wait_ge(s_vv.sem, gate_vv)
                for s in range(4):
                    PE.matmul(p_s[0:1, 512 * s:512 * (s + 1)], ONES[:, 0:1],
                              src_of(c)[:, 512 * s:512 * (s + 1)],
                              start=(c == 0), stop=(c == NCH - 1))
                PE.wait_ge(s_aa.sem, aa0 + c + 1)
                for s in range(4):
                    mm = PE.matmul(p_q[0:1, 512 * s:512 * (s + 1)],
                                   ONES[:, 0:1],
                                   scr[:, 512 * s:512 * (s + 1)],
                                   start=(c == 0), stop=(c == NCH - 1))
                s_mm.inc(mm)
            DV.wait_ge(s_mm.sem, s_mm.v)
            DV.tensor_copy(out=scr[0:1, :], in_=p_s[0:1, :])
            s_vv.inc(DV.tensor_copy(out=scr[32:33, :], in_=p_q[0:1, :]))
        p_rb = psum_scope.enter_context(
            nc.psum_tensor("p_rb" + tag, [128, L], F32))
        p_mb = psum_scope.enter_context(
            nc.psum_tensor("p_mb" + tag, [128, L], F32))
        PE.wait_ge(s_vv.sem, s_vv.v)
        for s in range(4):
            PE.matmul(p_rb[:, 512 * s:512 * (s + 1)], ONES[0:1, :],
                      scr[0:1, 512 * s:512 * (s + 1)], start=True, stop=True)
            mm = PE.matmul(p_mb[:, 512 * s:512 * (s + 1)], ONES[32:33, :],
                          scr[32:33, 512 * s:512 * (s + 1)], start=True,
                          stop=True)
        s_mm.inc(mm)
        DV.wait_ge(s_mm.sem, s_mm.v)
        DV.tensor_scalar(out=scr[:], in0=p_rb[:], scalar1=1.0 / D,
                         scalar2=None, op0=OP.mult)
        DV.drain()
        DV.tensor_tensor(out=xnb[:], in0=scr[:], in1=scr[:], op=OP.mult)
        DV.drain()
        s_vv.inc(DV.scalar_tensor_tensor(
            out=p_rb[:], in0=p_mb[:], scalar=1.0 / D,
            in1=xnb[:], op0=OP.mult, op1=OP.subtract))
        AC.wait_ge(s_vv.sem, s_vv.v)
        AC.activation(out=p_mb[:], in_=p_rb[:], func=AF.Ln, bias=1e-5)
        AC.drain()
        s_aa.inc(AC.activation(out=p_rb[:], in_=p_mb[:], func=AF.Exp,
                               scale=-0.5))
        DV.wait_ge(s_aa.sem, s_aa.v)
        s_vv.inc(DV.tensor_tensor(out=p_mb[:], in0=scr[:], in1=p_rb[:],
                                  op=OP.mult))
        return p_rb, p_mb, s_vv.v

    def emit_proj(wbf, bias_col, dst, p_pr, q2s=None, p_norm=None,
                  first_norm_gate=None):
        aa_b = s_aa.v
        vv_b = s_vv.v
        PE.wait_ge(s_vv.sem, vv_b)
        PE.wait_ge(s_aa.sem, aa_b)
        proj_mm = []
        for i, (o, t) in enumerate(GROUPS):
            if i >= 2:
                if p_norm is not None:
                    PE.wait_ge(s_aa.sem, aa_b + (i - 2) + 1)
                PE.wait_ge(s_vv.sem, vv_b + (i - 2) + 1)
            for c in range(NCH):
                mm = PE.matmul(
                    p_pr[i % 2][:],
                    wbf[:, D * c + 128 * o:D * c + 128 * (o + 1)],
                    xn[:, L * c + 512 * t:L * c + 512 * (t + 1)],
                    start=(c == 0), stop=(c == NCH - 1))
            s_mm.inc(mm)
            proj_mm.append(s_mm.v)
            if p_norm is not None and i >= 1:
                PE.wait_ge(s_aa.sem, aa_b + i)
                if i == 1 and first_norm_gate is not None:
                    PE.wait_ge(first_norm_gate[0], first_norm_gate[1])
                oo, tt = GROUPS[i - 1]
                mm = PE.matmul(p_norm[:, 512 * tt:512 * (tt + 1)],
                               B16[:, 16 * oo:16 * (oo + 1)],
                               q2s[(i - 1) % 2][:],
                               start=(oo == 0), stop=(oo == NCH - 1))
                s_mm.inc(mm)
        if p_norm is not None:
            PE.wait_ge(s_aa.sem, aa_b + len(GROUPS))
            oo, tt = GROUPS[-1]
            mm = PE.matmul(p_norm[:, 512 * tt:512 * (tt + 1)],
                           B16[:, 16 * oo:16 * (oo + 1)],
                           q2s[(len(GROUPS) - 1) % 2][:],
                           start=(oo == 0), stop=(oo == NCH - 1))
            s_mm.inc(mm)
            for i, (o, t) in enumerate(GROUPS):
                AC.wait_ge(s_mm.sem, proj_mm[i])
                s_aa.inc(AC.activation(out=q2s[i % 2][:], in_=p_pr[i % 2][:],
                                       func=AF.Square,
                                       bias=bias_col[:, o:o + 1]))
        for i, (o, t) in enumerate(GROUPS):
            DV.wait_ge(s_mm.sem, proj_mm[i])
            if p_norm is not None:
                DV.wait_ge(s_aa.sem, aa_b + i + 1)
            s_vv.inc(DV.tensor_scalar(
                out=dst[:, L * o + 512 * t:L * o + 512 * (t + 1)],
                in0=p_pr[i % 2][:], scalar1=bias_col[:, o:o + 1],
                scalar2=None, op0=OP.add))
        return s_vv.v

    scope1 = contextlib.ExitStack()
    if box is not None:
        box["scope1"] = scope1
    xn = scope1.enter_context(nc.sbuf_tensor("xn", [128, NCH * L], BF16))
    with contextlib.ExitStack() as esA:
        xs = esA.enter_context(nc.sbuf_tensor("xs", [128, NCH * L], F32))
        sqscrA = esA.enter_context(nc.sbuf_tensor("sqscrA", [128, L], F32))
        xv = xT.rearrange("(c p) t -> p c t", p=128)
        for c in range(NCH):
            s_lx[c].inc(SY.dma_start(out=xs[:, L * c:L * (c + 1)],
                                     in_=xv[:, c, :]))
        wq_done = emit_stage(wqT, wbfA, cast_plain)

        def x_gate(c, eng):
            eng.wait_ge(s_lx[c].sem, 16)

        with contextlib.ExitStack() as esLP:
            p_rb, p_mb, ln1_rdy = emit_ln(
                lambda c: xs[:, L * c:L * (c + 1)], 0, sqscrA,
                xn[:, 0:L], esLP, x_gate=x_gate)
            DV.wait_ge(s_vv.sem, ln1_rdy)
            DV.drain()
            for c in range(NCH):
                DV.tensor_tensor(out=sqscrA[:], in0=xs[:, L * c:L * (c + 1)],
                                 in1=p_rb[:], op=OP.mult)
                DV.drain()
                DV.tensor_tensor(out=xn[:, L * c:L * (c + 1)], in0=sqscrA[:],
                                 in1=p_mb[:], op=OP.subtract)
                s_vv.inc(DV.drain())
            ln1_done = s_vv.v
            if stop == 1:
                dbg_dump(xn[:, 0:4096], 128, 4096,
                         [(s_vv.sem, ln1_done)])

    with contextlib.ExitStack() as esB:
        qf = esB.enter_context(nc.sbuf_tensor("qf", [128, NCH * L], BF16))
        kf = esB.enter_context(nc.sbuf_tensor("kf", [128, NCH * L], BF16))
        vt = esB.enter_context(nc.sbuf_tensor("vt", [128, NTT * VAW], BF16))
        shscr = esB.enter_context(nc.sbuf_tensor("shscr", [128, L], F32))
        q2s = [shscr[:, 0:512], shscr[:, 512:1024]]
        srow = shscr[32:33, 0:512]
        negBr = shscr[64:65, 512:528]
        mask16 = esB.enter_context(nc.sbuf_tensor("mask16", [16, L], BF16))
        esPR = contextlib.ExitStack()
        p_pr = [esPR.enter_context(nc.psum_tensor(f"p_pr{i}", [128, 512], F32))
                for i in range(2)]

        wk_done = emit_stage(wkT, wbfB, cast_plain)
        if stop == 15:
            dbg_dump(wbfB[:, 0:4096], 128, 4096, [(s_vv.sem, wk_done)])
        if stop == 16:
            PE.wait_ge(s_vv.sem, max(wq_done, ln1_done))
            emit_proj(wbfA, sbq, qf, p_pr)
            dbg_dump(qf[:, 0:4096], 128, 4096, [(s_vv.sem, s_vv.v)])
        if stop == 18:
            q2sD = [esB.enter_context(nc.sbuf_tensor(f"q2sE{i}", [128, 512], F32)) for i in range(2)]
            PE.wait_ge(s_vv.sem, max(wq_done, ln1_done))
            aa_b = s_aa.v
            vv_b = s_vv.v
            proj_mm = []
            for i, (o, t) in enumerate(GROUPS):
                if i >= 2:
                    PE.wait_ge(s_aa.sem, aa_b + (i - 2) + 1)
                    PE.wait_ge(s_vv.sem, vv_b + (i - 2) + 1)
                for c in range(NCH):
                    mm = PE.matmul(
                        p_pr[i % 2][:],
                        wbfA[:, D * c + 128 * o:D * c + 128 * (o + 1)],
                        xn[:, L * c + 512 * t:L * c + 512 * (t + 1)],
                        start=(c == 0), stop=(c == NCH - 1))
                s_mm.inc(mm)
                proj_mm.append(s_mm.v)
            for i, (o, t) in enumerate(GROUPS):
                AC.wait_ge(s_mm.sem, proj_mm[i])
                s_aa.inc(AC.activation(out=q2sD[i % 2][:], in_=p_pr[i % 2][:],
                                       func=AF.Square, bias=sbq[:, o:o + 1]))
            for i, (o, t) in enumerate(GROUPS):
                DV.wait_ge(s_mm.sem, proj_mm[i])
                DV.tensor_scalar(out=qf[:, L * o + 512 * t:L * o + 512 * (t + 1)],
                                 in0=p_pr[i % 2][:], scalar1=sbq[:, o:o + 1],
                                 scalar2=None, op0=OP.add)
                s_vv.inc(DV.drain())
            dbg_dump(qf[:, 0:4096], 128, 4096,
                     [(s_vv.sem, s_vv.v), (s_aa.sem, s_aa.v)])
        if stop == 17:
            with contextlib.ExitStack() as esP:
                p_nrm = esP.enter_context(nc.psum_tensor("p_nrmD", [16, L], F32))
                q2sD = [esB.enter_context(nc.sbuf_tensor(f"q2sD{i}", [128, 512], F32)) for i in range(2)]
                PE.wait_ge(s_vv.sem, max(wq_done, ln1_done))
                emit_proj(wbfA, sbq, qf, p_pr, q2s=q2sD, p_norm=p_nrm,
                          first_norm_gate=(s_vv.sem, ln1_done))
                dbg_dump(qf[:, 0:4096], 128, 4096,
                         [(s_vv.sem, s_vv.v), (s_mm.sem, s_mm.v)])

        normsT = esB.enter_context(nc.sbuf_tensor("normsT", [16, L], F32))
        junk16 = mask16

        with contextlib.ExitStack() as esP:
            p_nrm = esP.enter_context(nc.psum_tensor("p_nrm", [16, L], F32))
            PE.wait_ge(s_vv.sem, max(wq_done, ln1_done))
            emit_proj(wbfA, sbq, qf, p_pr, q2s=q2s, p_norm=p_nrm,
                      first_norm_gate=(s_vv.sem, ln1_done))
            q_mm_all = s_mm.v
            DV.wait_ge(s_mm.sem, q_mm_all)
            DV.tensor_copy(out=normsT[:], in_=p_nrm[:])
            s_vv.inc(DV.drain())
            norms_ready = s_vv.v
            if stop == 2:
                dbg_dump(qf[:, 0:4096], 128, 4096,
                         [(s_vv.sem, s_vv.v)])
            if stop == 25:
                dbg_dump(normsT[:], 16, L, [(s_vv.sem, norms_ready)])

        with contextlib.ExitStack() as esP:
            p_kn = esP.enter_context(nc.psum_tensor("p_kn", [16, L], F32))
            PE.wait_ge(s_vv.sem, wk_done)
            emit_proj(wbfB, sbk, kf, p_pr, q2s=q2s, p_norm=p_kn,
                      first_norm_gate=(s_vv.sem, norms_ready))
            k_mm_all = s_mm.v
            DV.wait_ge(s_mm.sem, k_mm_all)
            DV.reduce_max(out=kmaxv[:], in_=p_kn[:], axis=AX.X)
            s_vv.inc(DV.drain())
            kmax_ready = s_vv.v
            if stop == 3:
                dbg_dump(kf[:, 0:4096], 128, 4096, [(s_vv.sem, s_vv.v)])
        esPR.close()

        DV.wait_ge(s_vv.sem, norms_ready)
        DV.reduce_max(out=bhi[0][:], in_=normsT[:], axis=AX.X)
        DV.memset(blo[0][:], 0.0)
        DV.drain()
        DV.tensor_copy(out=sqq[:], in_=bhi[0][:])
        DV.drain()

        def emit_bisect_iter(it):
            a, b = it % 2, (it + 1) % 2
            DV.tensor_scalar(out=btm[:], in0=blo[a][:], scalar1=bhi[a][:],
                             scalar2=0.5, op0=OP.add, op1=OP.mult)
            DV.drain()
            DV.tensor_scalar(out=junk16[:], in0=normsT[:], scalar1=btm[:],
                             scalar2=None, op0=OP.is_gt, op1=OP.add,
                             accum_out=bcount[:])
            DV.drain()
            DV.tensor_scalar(out=bcondf[:], in0=bcount[:], scalar1=409.5,
                             scalar2=None, op0=OP.is_ge)
            DV.tensor_scalar(out=bcondn[:], in0=bcount[:], scalar1=409.5,
                             scalar2=None, op0=OP.is_lt)
            DV.drain()
            DV.tensor_tensor(out=bt4[:, 0:1], in0=btm[:], in1=bcondf[:],
                             op=OP.mult)
            DV.tensor_tensor(out=bt4[:, 1:2], in0=blo[a][:], in1=bcondn[:],
                             op=OP.mult)
            DV.tensor_tensor(out=bt4[:, 2:3], in0=bhi[a][:], in1=bcondf[:],
                             op=OP.mult)
            DV.tensor_tensor(out=bt4[:, 3:4], in0=btm[:], in1=bcondn[:],
                             op=OP.mult)
            DV.drain()
            DV.tensor_tensor(out=blo[b][:], in0=bt4[:, 0:1], in1=bt4[:, 1:2],
                             op=OP.add)
            DV.tensor_tensor(out=bhi[b][:], in0=bt4[:, 2:3], in1=bt4[:, 3:4],
                             op=OP.add)
            DV.drain()

        def emit_bisect_final():
            fin = BITERS % 2
            DV.tensor_scalar(out=mask16[:], in0=normsT[:], scalar1=bhi[fin][:],
                             scalar2=None, op0=OP.is_gt)
            s_vv.inc(DV.drain())

        sums_hT = shscr[0:16, :]
        scale_hT = mask16

        DV.wait_ge(s_gd.sem, bva_gd)
        wv_done = emit_stage(wvT, wbfC, cast_vaug,
                             hook=lambda q: emit_bisect_iter(q))
        DV.wait_ge(s_vv.sem, wv_done)
        for c in range(NCH):
            zv = wbfC[:, VAW * c:VAW * (c + 1)].rearrange(
                "p (h w) -> p h w", h=H)[:, :, 64:65]
            DV.memset(zv, 0.0)
        s_vv.inc(DV.drain())
        vaug_ready = s_vv.v

        with contextlib.ExitStack() as esP:
            p_vs = [esP.enter_context(nc.psum_tensor(f"p_v{i}", [128, VAW], F32))
                    for i in range(2)]
            PE.wait_ge(s_vv.sem, max(vaug_ready, kmax_ready))
            vv_b = s_vv.v
            vmm = []
            for i in range(NTT):
                pv = p_vs[i % 2]
                if i >= 2:
                    PE.wait_ge(s_vv.sem, vv_b + (i - 2) + 1)
                for c in range(NCH):
                    for (n0, n1) in [(0, 512), (512, 1024), (1024, VAW)]:
                        PE.matmul(pv[:, n0:n1],
                                  xn[:, L * c + 128 * i:L * c + 128 * (i + 1)],
                                  wbfC[:, VAW * c + n0:VAW * c + n1],
                                  start=(c == 0), stop=False)
                for (n0, n1) in [(0, 512), (512, 1024), (1024, VAW)]:
                    mm = PE.matmul(pv[:, n0:n1], ONESBF[:, 0:128],
                                   bva[:, n0:n1], start=False, stop=True)
                s_mm.inc(mm)
                vmm.append(s_mm.v)
            for i in range(NTT):
                DV.wait_ge(s_mm.sem, vmm[i])
                DV.tensor_copy(out=vt[:, VAW * i:VAW * (i + 1)],
                               in_=p_vs[i % 2][:])
                s_vv.inc(DV.drain())
                if 8 + i < BITERS:
                    emit_bisect_iter(8 + i)
            for _it in range(8 + NTT, BITERS):
                emit_bisect_iter(_it)
            emit_bisect_final()
            if stop == 35:
                dbg_dump(mask16[:], 16, L, [(s_vv.sem, s_vv.v)])

        DV.wait_ge(s_vv.sem, s_vv.v)
        s_vv.inc(DV.tensor_scalar(out=negB[:], in0=sqq[:], scalar1=kmaxv[:],
                                  scalar2=SCALE * SCALE, op0=OP.mult,
                                  op1=OP.mult))
        AC.wait_ge(s_vv.sem, s_vv.v)
        s_aa.inc(AC.activation(out=negB[:], in_=negB[:], func=AF.Sqrt))
        with contextlib.ExitStack() as esP:
            p_bnd = esP.enter_context(nc.psum_tensor("p_bnd", [128, 512], F32))
            PE.wait_ge(s_aa.sem, s_aa.v)
            mm = PE.transpose(p_bnd[0:1, 0:16], negB[:], IDT[0:16, 0:16])
            s_mm.inc(mm)
            DV.wait_ge(s_mm.sem, s_mm.v)
            s_vv.inc(DV.tensor_copy(out=negBr[:], in_=p_bnd[0:1, 0:16]))
            PE.wait_ge(s_vv.sem, s_vv.v)
            mm = PE.matmul(p_bnd[:, 0:16], ONES[64:65, :], negBr[:],
                           start=True, stop=True)
            s_mm.inc(mm)
            DV.wait_ge(s_mm.sem, s_mm.v)
            s_vv.inc(DV.tensor_scalar(out=negBb[:], in0=p_bnd[:, 0:16],
                                      scalar1=-1.0, scalar2=None,
                                      op0=OP.mult))
        bound_ready = s_vv.v
        if stop == 4:
            dbg_dump(vt[:, 0:4096], 128, 4096, [(s_vv.sem, s_vv.v)])
        if stop == 5:
            dbg_dump(negBb[:], 128, 16, [(s_vv.sem, bound_ready)])

        wfv = wfT.rearrange("(c p) o -> p c o", p=128)
        f1_ld = {}
        f1_abs = {}
        f1_mm = {}

        wo_done = emit_stage(woT, wbfA, cast_plain,
                             gate=(s_mm.sem, q_mm_all))

        def emit_f1_chunk(c):
            if c == 0:
                SY.wait_ge(s_vv.sem, wo_done)
            else:
                SY.wait_ge(s_aa.sem, f1_abs[c - 1])
                SY.wait_ge(s_mm.sem, f1_mm[c - 1])
            s_ld.inc(SY.dma_start(out=wstage[:], in_=wfv[:, c, :]))
            f1_ld[c] = s_ld.v
            AC.wait_ge(s_ld.sem, f1_ld[c])
            s_aa.inc(AC.activation(out=wstage[:], in_=wstage[:], func=AF.Abs))
            f1_abs[c] = s_aa.v
            PE.wait_ge(s_aa.sem, f1_abs[c])
            for (n0, n1) in [(0, 512), (512, 1024)]:
                mm = PE.matmul(p_ws[0:1, n0:n1], ONES[:, 0:1],
                               wstage[:, n0:n1],
                               start=(c == 0), stop=(c == NCH - 1))
            s_mm.inc(mm)
            f1_mm[c] = s_mm.v

        expT = xn
        esWS = contextlib.ExitStack()
        p_ws = esWS.enter_context(nc.psum_tensor("p_ws", [1, 1024], F32))
        with contextlib.ExitStack() as esAT:
            p_sc = [esAT.enter_context(nc.psum_tensor(f"p_sc{i}", [128, 512], F32))
                    for i in range(4)]
            p_av = [esAT.enter_context(nc.psum_tensor(f"p_av{i}", [VW, 512], F32))
                    for i in range(2)]

            def attnC_ap(pair):
                reg = wbfB if pair < 4 else wbfC
                return reg[:, L * (pair % 4):L * (pair % 4 + 1)]

            PE.wait_ge(s_vv.sem, bound_ready)
            exp_aa = {}
            av_mm = {}
            av_vv = {}
            NIT = H * 4

            def emit_scores(it):
                h, s = it // 4, it % 4
                pair, r0 = h // 2, 64 * (h % 2)
                ebuf = expT[:, 8192 * (it % 2):8192 * (it % 2 + 1)]
                for j in range(NTT):
                    tile = j % 4
                    if j >= 4:
                        PE.wait_ge(s_aa.sem, exp_aa[(it, j - 4)])
                    elif it >= 1:
                        PE.wait_ge(s_aa.sem, exp_aa[(it - 1, 12 + j)])
                    mm = PE.matmul(
                        p_sc[tile][:],
                        kf[r0:r0 + 64,
                           L * pair + 128 * j:L * pair + 128 * (j + 1)],
                        qf[r0:r0 + 64,
                           L * pair + 512 * s:L * pair + 512 * (s + 1)],
                        start=True, stop=True)
                    s_mm.inc(mm)
                    AC.wait_ge(s_mm.sem, s_mm.v)
                    if it >= 2 and j == 0:
                        AC.wait_ge(s_mm.sem, av_mm[it - 2])
                    s_aa.inc(AC.activation(
                        out=ebuf[:, 512 * j:512 * (j + 1)], in_=p_sc[tile][:],
                        func=AF.Exp, scale=SCALE, bias=negBb[:, h:h + 1]))
                    exp_aa[(it, j)] = s_aa.v

            def emit_av(it):
                h, s = it // 4, it % 4
                pair, r0 = h // 2, 64 * (h % 2)
                ebuf = expT[:, 8192 * (it % 2):8192 * (it % 2 + 1)]
                if it >= 2:
                    PE.wait_ge(s_vv.sem, av_vv[it - 2])
                for j in range(NTT):
                    PE.wait_ge(s_aa.sem, exp_aa[(it, j)])
                    mm = PE.matmul(p_av[it % 2][:],
                                   vt[:, VAW * j + VW * h:VAW * j + VW * (h + 1)],
                                   ebuf[:, 512 * j:512 * (j + 1)],
                                   start=(j == 0), stop=(j == NTT - 1))
                s_mm.inc(mm)
                av_mm[it] = s_mm.v
                DV.wait_ge(s_mm.sem, s_mm.v)
                if it >= 1:
                    DV.wait_ge(s_gd.sem, s_gd.v)
                DV.tensor_copy(out=srow[:], in_=p_av[it % 2][64:65, :])
                s_vv.inc(DV.tensor_copy(
                    out=attnC_ap(pair)[r0:r0 + 64, 512 * s:512 * (s + 1)],
                    in_=p_av[it % 2][0:64, :]))
                av_vv[it] = s_vv.v
                GP.wait_ge(s_vv.sem, s_vv.v)
                s_gd.inc(GP.dma_start(
                    out=sums_hT[h:h + 1, 512 * s:512 * (s + 1)], in_=srow[:]))

            for it in range(NIT):
                emit_scores(it)
                if it >= 1:
                    emit_av(it - 1)
                if 2 <= it < 2 + NCH:
                    emit_f1_chunk(it - 2)
            emit_av(NIT - 1)

        with contextlib.ExitStack() as esP:
            p_m = esP.enter_context(nc.psum_tensor("p_m2", [128, 512], F32))
            DV.wait_ge(s_mm.sem, f1_mm[NCH - 1])
            DV.tensor_reduce(out=wsum[:], in_=p_ws[0:1, :], axis=AX.X,
                             op=OP.add)
            DV.drain()
            DV.tensor_scalar(out=thr1[:], in0=wsum[:], scalar1=0.5 / (D * D),
                             scalar2=None, op0=OP.mult)
            DV.drain()
            DV.tensor_scalar(out=wsum[:], in0=wsum[:], scalar1=1.0 / (D * D),
                             scalar2=None, op0=OP.mult)
            s_vv.inc(DV.drain())
            PE.wait_ge(s_vv.sem, s_vv.v)
            PE.matmul(p_m[:, 1:2], ONES[0:1, :], thr1[0:1, 0:1], start=True,
                      stop=True)
            mm = PE.matmul(p_m[:, 2:3], ONES[0:1, :], wsum[0:1, 0:1],
                           start=True, stop=True)
            s_mm.inc(mm)
            DV.wait_ge(s_mm.sem, s_mm.v)
            DV.tensor_copy(out=thrb[:], in_=p_m[:, 1:2])
            DV.tensor_copy(out=wscb[:], in_=p_m[:, 2:3])
            DV.tensor_scalar(out=nthrb[:], in0=p_m[:, 1:2], scalar1=-1.0,
                             scalar2=None, op0=OP.mult)
            s_vv.inc(DV.drain())
            thr_ready = s_vv.v
        esWS.close()

        if stop == 6:
            GP.wait_ge(s_vv.sem, s_vv.v)
            dbg_dump(wbfB[:, 0:4096], 128, 4096, [(s_gd.sem, s_gd.v)])
        if stop == 65:
            dbg_dump(sums_hT[:], 16, L, [(s_gd.sem, s_gd.v)])
        DV.wait_ge(s_gd.sem, s_gd.v)
        DV.reciprocal(out=sums_hT[:], in_=sums_hT[:])
        DV.drain()
        DV.tensor_tensor(out=scale_hT[:], in0=sums_hT[:], in1=mask16[:],
                         op=OP.mult)
        s_vv.inc(DV.drain())
        with contextlib.ExitStack() as esP:
            p_scl = [esP.enter_context(nc.psum_tensor(f"p_scl{i}", [128, 512],
                                                      F32)) for i in range(2)]
            PE.wait_ge(s_vv.sem, s_vv.v)
            scl_vv = []
            for i, (pair, s) in enumerate((p, s) for p in range(NCH)
                                          for s in range(4)):
                if i >= 2:
                    PE.wait_ge(s_vv.sem, scl_vv[i - 2])
                mm = PE.matmul(p_scl[i % 2][:],
                               cE[:, 128 * pair:128 * (pair + 1)],
                               scale_hT[:, 512 * s:512 * (s + 1)],
                               start=True, stop=True)
                s_mm.inc(mm)
                DV.wait_ge(s_mm.sem, s_mm.v)
                s_vv.inc(DV.tensor_tensor(
                    out=attnC_ap(pair)[:, 512 * s:512 * (s + 1)],
                    in0=attnC_ap(pair)[:, 512 * s:512 * (s + 1)],
                    in1=p_scl[i % 2][:], op=OP.mult))
                scl_vv.append(s_vv.v)
            PE.wait_ge(s_vv.sem, s_vv.v)
        attn_done = s_vv.v
        if stop == 7:
            dbg_dump(wbfB[:, 0:4096], 128, 4096, [(s_vv.sem, attn_done)])

    scope1.close()

    with contextlib.ExitStack() as esC:
        x1 = esC.enter_context(nc.sbuf_tensor("x1", [128, NCH * L], F32))
        xrld = [esC.enter_context(nc.sbuf_tensor(f"xrld{i}", [128, 512], F32))
                for i in range(2)]
        x2st = esC.enter_context(nc.sbuf_tensor("x2st", [128, L], F32))
        rstd2 = esC.enter_context(nc.sbuf_tensor("rstd2", [128, L], F32))
        mr2 = esC.enter_context(nc.sbuf_tensor("mr2", [128, L], F32))
        qbf = esC.enter_context(nc.sbuf_tensor("qbf", [128, NCH * L], BF16))
        sfb = esC.enter_context(nc.sbuf_tensor("sfb", [1, D], F32))
        badjb = esC.enter_context(nc.sbuf_tensor("badjb", [1, D], BF16))
        lmax8 = esC.enter_context(nc.sbuf_tensor("lmax8", [128, NCH], F32))
        lmax1 = esC.enter_context(nc.sbuf_tensor("lmax1", [128, 1], F32))
        lmaxr = esC.enter_context(nc.sbuf_tensor("lmaxr", [1, 128], F32))
        cmax = esC.enter_context(nc.sbuf_tensor("cmax", [128, 1], F32))
        gmaxv = esC.enter_context(nc.sbuf_tensor("gmaxv", [128, 1], F32))
        recipb = esC.enter_context(nc.sbuf_tensor("recipb", [128, 1], F32))
        wsisb = esC.enter_context(nc.sbuf_tensor("wsisb", [128, 1], F32))
        isv = esC.enter_context(nc.sbuf_tensor("isv", [128, 1], F32))
        wsis1 = esC.enter_context(nc.sbuf_tensor("wsis1", [1, 1], F32))
        rws = esC.enter_context(nc.sbuf_tensor("rws", [1, 1], F32))
        wabs = x2st[:, 0:D]
        rr_b = rstd2
        mrr_b = mr2

        SY.wait_ge(s_vv.sem, attn_done)
        s_lf.inc(SY.dma_start(out=sfb[:], in_=fb_r))
        sfb_ld = s_lf.v

        def attnC_ap(pair):
            reg = wbfB if pair < 4 else wbfC
            return reg[:, L * (pair % 4):L * (pair % 4 + 1)]

        with contextlib.ExitStack() as esP:
            p_w = [esP.enter_context(nc.psum_tensor(f"p_w{i}", [128, 512], F32))
                   for i in range(2)]
            xv = xT.rearrange("(c p) t -> p c t", p=128)
            pass
            vv_b = s_vv.v
            for i, (o, t) in enumerate(GROUPS):
                SY.wait_ge(s_vv.sem, vv_b if i < 2 else vv_b + (i - 2) + 1)
                s_lr[i % 2].inc(SY.dma_start(
                    out=xrld[i % 2][:],
                    in_=xv[:, o, 512 * t:512 * (t + 1)]))
            PE.wait_ge(s_vv.sem, max(attn_done, wo_done))
            wo_mm = []
            for i, (o, t) in enumerate(GROUPS):
                if i >= 2:
                    PE.wait_ge(s_vv.sem, vv_b + (i - 2) + 1)
                for c in range(NCH):
                    mm = PE.matmul(
                        p_w[i % 2][:],
                        wbfA[:, D * c + 128 * o:D * c + 128 * (o + 1)],
                        attnC_ap(c)[:, 512 * t:512 * (t + 1)],
                        start=(c == 0), stop=(c == NCH - 1))
                s_mm.inc(mm)
                wo_mm.append(s_mm.v)
            for i, (o, t) in enumerate(GROUPS):
                DV.wait_ge(s_mm.sem, wo_mm[i])
                DV.wait_ge(s_lr[i % 2].sem, 16 * (i // 2 + 1))
                s_vv.inc(DV.scalar_tensor_tensor(
                    out=x1[:, L * o + 512 * t:L * o + 512 * (t + 1)],
                    in0=p_w[i % 2][:], scalar=sbo[:, o:o + 1],
                    in1=xrld[i % 2][:], op0=OP.add, op1=OP.add))
            x1_done = s_vv.v
            wo_mm_all = s_mm.v
            if stop == 8:
                dbg_dump(x1[:, 0:4096], 128, 4096, [(s_vv.sem, x1_done)])

        with contextlib.ExitStack() as esLP:
            p_rb, p_mb, ln2_rdy = emit_ln(
                lambda c: x1[:, L * c:L * (c + 1)], x1_done, x2st,
                qbf[:, 0:L], esLP)
            DV.wait_ge(s_vv.sem, ln2_rdy)
            DV.tensor_copy(out=rstd2[:], in_=p_rb[:])
            s_vv.inc(DV.tensor_copy(out=mr2[:], in_=p_mb[:]))
            DV.drain()
            rstd2_done = s_vv.v

        for c in range(NCH):
            DV.tensor_tensor(out=x2st[:], in0=x1[:, L * c:L * (c + 1)],
                             in1=rstd2[:], op=OP.mult)
            DV.drain()
            DV.tensor_tensor(out=x2st[:], in0=x2st[:], in1=mr2[:],
                             op=OP.subtract)
            DV.drain()
            DV.tensor_reduce(out=lmax8[:, c:c + 1], in_=x2st[:], axis=AX.X,
                             op=OP.max, apply_absolute_value=True)
            DV.drain()
        DV.tensor_reduce(out=lmax1[:], in_=lmax8[:], axis=AX.X, op=OP.max)
        s_vv.inc(DV.drain())
        with contextlib.ExitStack() as esP:
            p_m = esP.enter_context(nc.psum_tensor("p_m1", [128, 512], F32))
            PE.wait_ge(s_vv.sem, s_vv.v)
            mm = PE.transpose(p_m[0:1, 0:128], lmax1[:], IDT[:, 0:128])
            s_mm.inc(mm)
            DV.wait_ge(s_mm.sem, s_mm.v)
            DV.tensor_copy(out=lmaxr[:], in_=p_m[0:1, 0:128])
            DV.drain()
            DV.tensor_reduce(out=wsum[:], in_=lmaxr[:], axis=AX.X, op=OP.max)
            s_vv.inc(DV.drain())
            PE.wait_ge(s_vv.sem, s_vv.v)
            mm = PE.matmul(p_m[:, 0:1], ONES[0:1, :], wsum[0:1, 0:1],
                           start=True, stop=True)
            s_mm.inc(mm)
            DV.wait_ge(s_mm.sem, s_mm.v)
            DV.tensor_copy(out=cmax[:], in_=p_m[:, 0:1])
            s_vv.inc(DV.drain())
            cmax_done = s_vv.v

        GP.wait_ge(s_vv.sem, cmax_done)
        s_gd.inc(GP.dma_start(out=cc_in[:, :], in_=cmax[:]))
        GP.wait_ge(s_gd.sem, s_gd.v)
        GP.collective_compute(
            "AllReduce", OP.max, replica_groups=[list(range(NCORES))],
            ins=[cc_in.ap().opt()], outs=[cc_out.ap().opt()],
        ).then_inc(s_cc.sem, 1)
        s_cc.v += 1
        GP.wait_ge(s_cc.sem, s_cc.v)
        s_gd.inc(GP.dma_start(out=gmaxv[:], in_=cc_out[:, :]))
        gmax_gd = s_gd.v

        vvp0 = s_vv.v
        for c in range(NCH):
            SY.wait_ge(s_vv.sem, s_vv.v if c == 0 else vvp0 + c)
            s_ld.inc(SY.dma_start(out=wstage[:], in_=wfv[:, c, :]))
            DV.wait_ge(s_ld.sem, s_ld.v)
            if c == 0:
                DV.wait_ge(s_vv.sem, thr_ready)
                DV.wait_ge(s_mm.sem, wo_mm_all)
            DV.tensor_scalar(out=x2st[:, 0:D], in0=wstage[:], scalar1=thrb[:],
                             scalar2=None, op0=OP.is_gt)
            DV.tensor_scalar(out=x2st[:, D:2 * D], in0=wstage[:],
                             scalar1=nthrb[:], scalar2=None, op0=OP.is_lt)
            DV.drain()
            DV.tensor_tensor(out=wbfA[:, D * c:D * (c + 1)],
                             in0=x2st[:, 0:D], in1=x2st[:, D:2 * D],
                             op=OP.subtract)
            s_vv.inc(DV.drain())
        qwt_done = s_vv.v

        DV.wait_ge(s_gd.sem, gmax_gd)
        DV.wait_ge(s_lf.sem, sfb_ld)
        DV.reciprocal(out=recipb[:], in_=gmaxv[:])
        DV.drain()
        DV.tensor_scalar(out=recipb[:], in0=recipb[:], scalar1=127.0,
                         scalar2=None, op0=OP.mult)
        DV.tensor_scalar(out=isv[:], in0=gmaxv[:], scalar1=1.0 / 127.0,
                         scalar2=None, op0=OP.mult)
        DV.drain()
        DV.tensor_tensor(out=wsisb[:], in0=isv[:], in1=wscb[:], op=OP.mult)
        DV.drain()
        DV.tensor_copy(out=wsis1[:], in_=wsisb[0:1, 0:1])
        DV.drain()
        DV.reciprocal(out=rws[:], in_=wsis1[:])
        DV.drain()
        DV.tensor_scalar(out=badjb[:], in0=sfb[:], scalar1=rws[:], scalar2=None,
                         op0=OP.mult)
        DV.drain()
        DV.tensor_scalar(out=rr_b[:], in0=rstd2[:], scalar1=recipb[:],
                         scalar2=None, op0=OP.mult)
        DV.drain()
        DV.tensor_scalar(out=mrr_b[:], in0=mr2[:], scalar1=recipb[:],
                         scalar2=None, op0=OP.mult)
        s_vv.inc(DV.drain())

        if stop == 85:
            dbg_dump(badjb[:], 1, 1024, [(s_vv.sem, s_vv.v)])
        if stop == 86:
            dbg_dump(sfb[:], 1, 1024, [(s_vv.sem, s_vv.v)])
        qi32 = esC.enter_context(nc.sbuf_tensor("qi32", [128, L], I32))
        x2stB = esC.enter_context(nc.sbuf_tensor("x2stB", [128, L], F32))
        aa_qz = s_aa.v
        for c in range(NCH):
            if c >= 1:
                DV.wait_ge(s_aa.sem, aa_qz + c)
            DV.tensor_tensor(out=x2st[:], in0=x1[:, L * c:L * (c + 1)],
                             in1=rr_b[:], op=OP.mult)
            DV.drain()
            DV.tensor_tensor(out=qi32[:], in0=x2st[:], in1=mrr_b[:],
                             op=OP.subtract)
            s_vv.inc(DV.drain())
            AC.wait_ge(s_vv.sem, s_vv.v)
            s_aa.inc(AC.copy(out=qbf[:, L * c:L * (c + 1)], in_=qi32[:]))
        quant_done = s_aa.v
        if stop == 9:
            dbg_dump(qbf[:, 0:4096], 128, 4096, [(s_aa.sem, quant_done)])
        if stop == 95:
            dbg_dump(gmaxv[:], 128, 1, [(s_aa.sem, quant_done)])

        with contextlib.ExitStack() as esP:
            p_f = [esP.enter_context(nc.psum_tensor(f"p_f{i}", [128, 512],
                                                    F32)) for i in range(8)]
            PE.wait_ge(s_vv.sem, qwt_done)
            yv = yT.rearrange("(c p) t -> p c t", p=128)
            st_base = s_st.v
            evac_last = {}
            for g in range(4):
                for c in range(NCH):
                    PE.wait_ge(s_aa.sem, aa_qz + c + 1)
                    for k, oo in enumerate((2 * g, 2 * g + 1)):
                        for t in range(4):
                            if g >= 1 and c == 0:
                                PE.wait_ge(s_vv.sem,
                                           evac_last[g - 1] - 7 + 4 * k + t)
                            PE.matmul(
                                p_f[4 * k + t][:],
                                wbfA[:, D * c + 128 * oo:D * c + 128 * (oo + 1)],
                                qbf[:, L * c + 512 * t:L * c + 512 * (t + 1)],
                                start=(c == 0), stop=False)
                for k, oo in enumerate((2 * g, 2 * g + 1)):
                    for t in range(4):
                        mm = PE.matmul(p_f[4 * k + t][:],
                                       badjb[:, 128 * oo:128 * (oo + 1)],
                                       ONESBF[:, 0:512], start=False,
                                       stop=True)
                        s_mm.inc(mm)
                mm_done = s_mm.v
                xbuf = [x2st, x2stB]
                for k, oo in enumerate((2 * g, 2 * g + 1)):
                    for t in range(4):
                        DV.wait_ge(s_mm.sem, mm_done - 7 + 4 * k + t)
                        if g >= 1 and k == 0 and t == 0:
                            DV.wait_ge(s_st.sem, st_base + 32 * g)
                        s_vv.inc(DV.scalar_tensor_tensor(
                            out=xbuf[k][:, 512 * t:512 * (t + 1)],
                            in0=p_f[4 * k + t][:], scalar=wsisb[:],
                            in1=x1[:, L * oo + 512 * t:L * oo + 512 * (t + 1)],
                            op0=OP.mult, op1=OP.add))
                evac_last[g] = s_vv.v
                for k, oo in enumerate((2 * g, 2 * g + 1)):
                    SY.wait_ge(s_vv.sem, evac_last[g] - 4 * (1 - k))
                    s_st.inc(SY.dma_start(out=yv[:, oo, :], in_=xbuf[k][:]))
            SY.wait_ge(s_st.sem, s_st.v)

    es.close()
    return nc


def _make_consts():
    import ml_dtypes
    c32 = np.zeros((128, 392), np.float32)
    c32[:, 0:128] = 1.0
    for c in range(NCH):
        for h in range(16):
            if h // 2 == c:
                ps = 64 * (h % 2)
                c32[ps:ps + 64, 128 + 16 * c + h] = 1.0
    c32[:, 256:384] = np.eye(128, dtype=np.float32)
    c32[:, 385] = 1e-5
    c32[:, 386] = -0.5
    c32[:, 387] = SCALE
    cE = np.zeros((16, 1024), np.float32)
    for c in range(NCH):
        for h in range(16):
            if h // 2 == c:
                ps = 64 * (h % 2)
                cE[h, 128 * c + ps:128 * c + ps + 64] = 1.0
    return c32, cE.astype(ml_dtypes.bfloat16)


_NC_CACHE = {}


def kernel(L0_attn_x, L0_attn_Wq, L0_attn_bq, L0_attn_Wk, L0_attn_bk,
           L0_attn_Wv, L0_attn_bv, L0_attn_Wo, L0_attn_bo,
           L0_ffn_weight, L0_ffn_bias):
    x = np.asarray(L0_attn_x, np.float32)
    c32, cE = _make_consts()

    def tr(w):
        return np.ascontiguousarray(np.asarray(w, np.float32).T)

    def cols(b):
        return np.ascontiguousarray(
            np.asarray(b, np.float32).reshape(NCH, 128).T)

    common = {
        "wqT": tr(L0_attn_Wq), "wkT": tr(L0_attn_Wk), "wvT": tr(L0_attn_Wv),
        "woT": tr(L0_attn_Wo), "wfT": tr(L0_ffn_weight),
        "bq_c": cols(L0_attn_bq), "bk_c": cols(L0_attn_bk),
        "bo_c": cols(L0_attn_bo),
        "bv_r": np.asarray(L0_attn_bv, np.float32).reshape(1, D),
        "fb_r": np.asarray(L0_ffn_bias, np.float32).reshape(1, D),
        "cst32": c32, "cstE": cE,
    }
    in_maps = []
    for b in range(NCORES):
        m = dict(common)
        m["xT"] = np.ascontiguousarray(x[b].T)
        in_maps.append(m)

    import os
    stop = int(os.environ.get("KSTOP", "99"))
    if "nc" not in _NC_CACHE:
        _NC_CACHE["nc"] = build_nc(stop)
    nc = _NC_CACHE["nc"]
    res = run_bass_kernel_spmd(nc, in_maps, core_ids=list(range(NCORES)))
    out = np.stack([np.ascontiguousarray(res.results[b]["yT"].T)
                    for b in range(NCORES)], axis=0)
    return out


def build_nc(stop=99):
    box = {}
    try:
        return _build_nc_inner(stop, box)
    except Exception as e:
        if type(e).__name__ == "StopEmit":
            return box["nc"]
        raise
